# version 10
# baseline (speedup 1.0000x reference)
import numpy as np
from contextlib import ExitStack

import concourse.bass as bass
import concourse.mybir as mybir
from concourse.bass_utils import run_bass_kernel_spmd

F32 = mybir.dt.float32
BF16 = mybir.dt.bfloat16
I32 = mybir.dt.int32
ALU = mybir.AluOpType
AF = mybir.ActivationFunctionType
AX = mybir.AxisListType.X

B, T, D, H, HD, DF = 2, 2048, 1024, 16, 64, 3072
NTOK = B * T
RD, RF = D // 128, DF // 128
TC = 512
NTCB = T // TC
NKC = T // 128
NQT = T // TC
NTC = NTOK // TC
QMAX = 32767.0
VBLK = 193
VROW = (H // 2) * VBLK

ENGS = ("sync", "scalar", "vector", "tensor", "gpsimd")

DBG = {}
PHASES = 99
S0 = 10**9


class Sem:
    __slots__ = ("name", "handle", "count")

    def __init__(self, name):
        self.name, self.handle, self.count = name, None, 0


SELF_CHAIN = ("vector", "scalar", "gpsimd")
SEM_OWNER = {"s_ev": "vector", "s_ak": "scalar", "s_gp": "gpsimd", "s_cc": "gpsimd"}


class KB:
    def __init__(self):
        self.streams = {e: [] for e in ENGS}
        self.sems = []
        self.selfsems = {e: None for e in SELF_CHAIN}
        self.selfcnt = {e: 0 for e in SELF_CHAIN}
        self.semmap = {}

    def sem(self, name):
        s = Sem(name)
        self.sems.append(s)
        return s

    def emit(self, eng, fn, wait=(), inc=(), inc16=None):
        chained = eng in SELF_CHAIN and inc16 is None
        if chained and fn is not None:
            self.selfcnt[eng] += 1
        for (s, n) in inc:
            old = s.count
            s.count += n
            if SEM_OWNER.get(s.name) is not None:
                own = SEM_OWNER[s.name]
                assert own == eng, (s.name, eng)
                for v in range(old + 1, s.count + 1):
                    self.semmap[(s.name, v)] = self.selfcnt[own]
        if inc16 is not None:
            inc16.count += 16
        self.streams[eng].append((fn, tuple(wait), tuple(inc), inc16))

    def xwait(self, e, s, v):
        own = SEM_OWNER.get(s.name)
        if own is not None:
            sc = self.semmap.get((s.name, v))
            if sc and sc > 0:
                e.wait_ge(self.selfsems[own].handle, sc)
        else:
            e.wait_ge(s.handle, v)

    def replay(self, eng_name, e):
        last_ins = None
        selfsem = self.selfsems.get(eng_name)
        nself = 0
        for (fn, waits, incs, inc16) in self.streams[eng_name]:
            for (s, v) in waits:
                if v > 0:
                    self.xwait(e, s, v)
            if selfsem is not None and inc16 is None and nself > 0:
                e.wait_ge(selfsem.handle, nself)
            ins = fn(e) if fn is not None else None
            if ins is not None:
                last_ins = ins
                if selfsem is not None and inc16 is None:
                    ins.then_inc(selfsem.handle, 1)
                    nself += 1
            keep = [(s, n) for (s, n) in incs if SEM_OWNER.get(s.name) is None]
            if keep or inc16 is not None:
                tgt = ins
                if tgt is None:
                    tgt = last_ins if last_ins is not None else e.nop()
                for (s, n) in keep:
                    tgt = tgt.then_inc(s.handle, n)
                if inc16 is not None:
                    tgt = tgt.then_inc(inc16.handle, 16)


class Gate:
    __slots__ = ("sc",)

    def __init__(self):
        self.sc = []

    def set(self, sem):
        self.sc = [(sem, sem.count)]

    def set_many(self, sems):
        self.sc = [(s, s.count) for s in sems]

    def w(self):
        return [(s, v) for (s, v) in self.sc if v > 0]


def build_nc(n_dbg=0):
    nc = bass.Bass("TRN2", target_bir_lowering=False)
    kb = KB()

    x_in = nc.dram_tensor("x", [B, T, D], F32, kind="ExternalInput").ap()
    out_hbm = nc.dram_tensor("out", [B, T, D], F32, kind="ExternalOutput").ap()
    wT_h = {k: nc.dram_tensor(k, [D, D], F32, kind="ExternalInput").ap()
            for k in ("wqT", "wkT", "wvT", "woT")}
    biases_h = nc.dram_tensor("biases", [4, D], F32, kind="ExternalInput").ap()
    wupT_h = nc.dram_tensor("wupT", [D, DF], F32, kind="ExternalInput").ap()
    wgT_h = nc.dram_tensor("wgT", [D, DF], F32, kind="ExternalInput").ap()
    wdT_h = nc.dram_tensor("wdT", [DF, D], F32, kind="ExternalInput").ap()
    wn_h = {"up": nc.dram_tensor("wup_n", [DF, D], F32, kind="ExternalInput").ap(),
            "g": nc.dram_tensor("wg_n", [DF, D], F32, kind="ExternalInput").ap(),
            "d": nc.dram_tensor("wd_n", [D, DF], F32, kind="ExternalInput").ap()}
    wmod_h = nc.dram_tensor("wmod", [D, 6 * D], F32, kind="ExternalInput").ap()
    c_in = nc.dram_tensor("c", [B, D], F32, kind="ExternalInput").ap()

    xfm_sp = nc.dram_tensor("xfm_sp", [RD, 128, NTOK], F32).ap()
    x1_sp = nc.dram_tensor("x1_sp", [RD, 128, NTOK], F32).ap()
    x1b_sp = nc.dram_tensor("x1b_sp", [RD, 128, NTOK], BF16).ap()
    z2_sp = nc.dram_tensor("z2_sp", [RD, 128, NTOK], BF16).ap()
    q_sp = nc.dram_tensor("q_sp", [B, RD, 128, T], BF16).ap()
    u_sp = nc.dram_tensor("u_sp", [RF, 128, NTOK], BF16).ap()
    g_sp = nc.dram_tensor("g_sp", [RF, 128, NTOK], BF16).ap()
    g2_sp = nc.dram_tensor("g2_sp", [RF, 128, NTOK], BF16).ap()
    op_sp = nc.dram_tensor("op_sp", [RF, 128, NTOK], BF16).ap()
    wq_sp = {"up": nc.dram_tensor("wupq_sp", [RD, 128, DF], BF16).ap(),
             "g": nc.dram_tensor("wgq_sp", [RD, 128, DF], BF16).ap(),
             "d": nc.dram_tensor("wdq_sp", [RF, 128, D], BF16).ap()}
    cc_in = [nc.dram_tensor(f"cc_in{i}", [1, 2 if i == 1 else 1], F32) for i in range(4)]
    cc_out = [nc.dram_tensor(f"cc_out{i}", [1, 2 if i == 1 else 1], F32) for i in range(4)]
    dbg = [nc.dram_tensor(f"dbg{i}", [128, 4096], F32, kind="ExternalOutput").ap()
           for i in range(n_dbg)]

    A = nc.alloc_sbuf_tensor
    ones_bf = A("ones_bf", [128, 128], BF16)
    one_f = A("one_f", [128, 128], F32)
    ident_f = A("ident_f", [128, 128], F32)
    ident_bf = A("ident_bf", [128, 128], BF16)
    evar3 = A("evar3", [96, RD * 128], BF16)
    consts = A("consts", [128, 4], F32)
    smod = A("smod", [128, 6 * B * RD], F32)
    sbias_c = A("sbias_c", [128, 4 * RD], F32)
    brow = A("brow", [1, D], BF16)
    sscal = A("sscal", [128, 16], F32)
    qfold = A("qfold", [128, 2 * B * RD], F32)
    stat = A("stat", [128, 32], F32)
    macc = A("macc", [128, 8], F32)
    AB = A("AB", [128, 77952], BF16)
    AFr = A("AFm", [128, 8192], F32)
    AI = A("AIm", [128, 1024], I32)

    s4 = nc.alloc_psum_tensor("s4", [128, 2048], F32)
    poa = nc.alloc_psum_tensor("poa", [128, 512], F32)
    pob = nc.alloc_psum_tensor("pob", [128, 512], F32)
    pzt = nc.alloc_psum_tensor("pzt", [128, 1024], BF16)
    ptf = nc.alloc_psum_tensor("ptf", [128, 512], F32)

    s_ld = kb.sem("s_ld")
    s_gld = kb.sem("s_gld")
    D_ = {name: kb.sem("d_" + name) for name in (
        "tap", "br", "bias", "c", "wm", "wq", "wk", "wv", "wo",
        "x0", "x1", "xs", "qs", "xr", "q0", "q1", "x1s", "x1b",
        "nat", "wt", "qw", "cc", "ba0", "ba1", "zs", "wu", "wg",
        "zr0", "zr1", "us", "gs", "gr0", "gr1", "g2s", "g2s2", "dd0", "dd1",
        "du0", "du1", "ops", "ops2", "wd", "or0", "or1", "x1r", "out",
        "nat2", "wt2", "qw2")}
    s_mm = kb.sem("s_mm")
    s_ev = kb.sem("s_ev")
    s_ak = kb.sem("s_ak")
    s_gp = kb.sem("s_gp")
    s_cc = kb.sem("s_cc")

    def W(s):
        return (s, s.count)

    ee = kb.emit

    def ab(o, n):
        return AB[:, o:o + n]

    def af(o, n):
        return AFr[:, o:o + n]

    def modcol(kind, b, rc):
        return kind * (B * RD) + b * RD + rc

    def tap(name, src_ap):
        if name not in DBG:
            return
        i = DBG[name]
        n = src_ap.shape[-1]
        ee("gpsimd", lambda e, i=i, n=n, src_ap=src_ap: e.dma_start(
            out=dbg[i][:, 0:n], in_=src_ap),
            wait=[W(s_mm), W(s_ak), W(s_ev), W(s_gp)], inc16=D_["tap"])
        ee("gpsimd", None, wait=[W(D_["tap"])])

    g_s4 = [Gate() for _ in range(4)]
    g_poa, g_pob, g_pzt, g_ptf = Gate(), Gate(), Gate(), Gate()
    g_af_x, g_af_s = Gate(), Gate()
    g_h = [Gate(), Gate()]
    g_qz = Gate()
    g_w = {k: Gate() for k in ("wqT", "wkT", "wvT", "woT")}
    g_wqo_busy = Gate()
    g_kfm, g_va = Gate(), Gate()
    g_pslot = [[Gate(), Gate()], [Gate(), Gate()]]
    g_ofm, g_tsb, g_bcs = Gate(), Gate(), Gate()

    ee("vector", lambda e: e.memset(ones_bf[:], 1.0))
    ee("vector", lambda e: e.memset(one_f[:], 1.0))
    ee("vector", lambda e: e.memset(consts[:, 0:1], -50.0))
    ee("vector", lambda e: e.memset(consts[:, 1:2], 1e-5))
    ee("vector", lambda e: e.memset(macc[:], 0.0), inc=[(s_ev, 1)])
    g_consts = Gate(); g_consts.set(s_ev)
    ee("gpsimd", lambda e: e.memset(ident_f[:], 1.0), inc=[(s_gp, 1)])
    ee("gpsimd", lambda e: e.affine_select(ident_f[:], ident_f[:], [[1, 128]],
                                           ALU.is_equal, 0.0, channel_multiplier=-1),
       wait=[W(s_gp)])
    ee("gpsimd", lambda e: e.memset(ident_bf[:], 1.0), inc=[(s_gp, 1)])
    ee("gpsimd", lambda e: e.affine_select(ident_bf[:], ident_bf[:], [[1, 128]],
                                           ALU.is_equal, 0.0, channel_multiplier=-1),
       wait=[W(s_gp)])
    ee("gpsimd", lambda e: e.memset(evar3[:], 1.0), inc=[(s_gp, 1)])
    for blk in range(3):
        ee("gpsimd", lambda e, blk=blk: e.affine_select(
            evar3[32 * blk:32 * (blk + 1), :], evar3[32 * blk:32 * (blk + 1), :],
            [[-4, RD], [-1, 4], [0, 32]], ALU.is_equal, 0.0, channel_multiplier=1),
            wait=[W(s_gp)])
    ee("gpsimd", None, inc=[(s_gp, 1)])
    g_idents = Gate(); g_idents.set(s_gp)

    if S0 < 1:
        return _finalize(nc, kb)
    ee("gpsimd", lambda e: e.dma_start(out=brow[:], in_=biases_h[2:3, :]), inc16=D_["br"])
    g_brow = Gate(); g_brow.set(D_["br"])
    ee("sync", lambda e: e.dma_start(out=AFr[0:4, 0:D], in_=biases_h), inc16=D_["bias"])
    ee("tensor", None, wait=[W(D_["bias"])] + g_idents.w())
    for rc in range(RD):
        ee("tensor", lambda e, rc=rc: e.transpose(
            ptf[:, rc * 4:(rc + 1) * 4],
            AFr[0:4, rc * 128:(rc + 1) * 128], ident_f[0:4, 0:4]))
    ee("tensor", None, inc=[(s_mm, 1)])
    ee("scalar", lambda e: e.copy(sbias_c[:], ptf[:, 0:4 * RD]),
       wait=[W(s_mm)], inc=[(s_ak, 1)])
    g_bias = Gate(); g_bias.set(s_ak)

    if S0 < 2:
        return _finalize(nc, kb)
    ee("vector", lambda e: e.memset(AFr[0:4, D:2 * D], 0.0),
       wait=g_bias.w(), inc=[(s_ev, 1)])
    ee("sync", lambda e: e.dma_start(out=AFr[0:B, D:2 * D], in_=c_in),
       wait=[W(s_ev)], inc16=D_["c"])
    ee("scalar", lambda e: e.activation(AFr[0:B, D:2 * D], AFr[0:B, D:2 * D], AF.Silu),
       wait=[W(D_["c"])], inc=[(s_ak, 1)])
    CFM = 60000
    g_cfm = Gate()
    if S0 < 21:
        return _finalize(nc, kb)
    ee("tensor", None, wait=[W(s_ak)] + g_bias.w())
    for rc in range(RD):
        ee("tensor", lambda e, rc=rc: e.transpose(
            ptf[:, 32 + rc * 4:32 + (rc + 1) * 4],
            AFr[0:4, D + rc * 128:D + (rc + 1) * 128], ident_f[0:4, 0:4]))
    ee("tensor", None, inc=[(s_mm, 1)])
    if S0 < 22:
        return _finalize(nc, kb)
    ee("scalar", lambda e: e.copy(ab(CFM, 4 * RD), ptf[:, 32:32 + 4 * RD]),
       wait=[W(s_mm)], inc=[(s_ak, 1)])
    g_cfm.set(s_ak)
    tap("cfm", ab(CFM, 32))
    tap("csilu", AFr[:, D:2 * D])
    tap("sbias", sbias_c[:, 0:32])

    if S0 < 23:
        return _finalize(nc, kb)
    for ic in range(RD):
        ee("gpsimd", lambda e, ic=ic: e.dma_start(
            out=ab(ic * 6144, 6144), in_=wmod_h[ic * 128:(ic + 1) * 128, :]),
            inc16=D_["wm"])
    gld_wmod = D_["wm"].count
    ee("tensor", None, wait=[(D_["wm"], gld_wmod)] + g_cfm.w())
    if S0 >= 231:
        for oc in range(6 * RD):
            for ic in range(RD):
                ee("tensor", lambda e, ic=ic, oc=oc: e.matmul(
                    s4[:, oc * 4:(oc + 1) * 4],
                    ab(ic * 6144, 6144)[:, oc * 128:(oc + 1) * 128],
                    ab(CFM, 4 * RD)[:, ic * 4:(ic + 1) * 4],
                    start=(ic == 0), stop=(ic == RD - 1)))
    ee("tensor", None, inc=[(s_mm, 1)])
    if S0 < 232:
        return _finalize(nc, kb)
    ee("scalar", lambda e: e.copy(AFr[:, 2048:2048 + 192], s4[:, 0:192]),
       wait=[W(s_mm)], inc=[(s_ak, 1)])
    if S0 < 2315:
        return _finalize(nc, kb)
    ee("vector", None, wait=[W(s_ak)])
    for kind in range(6):
        for rc in range(RD):
            oc = kind * RD + rc
            for b_ in range(B):
                ee("vector", lambda e, kind=kind, rc=rc, oc=oc, b_=b_: e.tensor_copy(
                    smod[:, modcol(kind, b_, rc):modcol(kind, b_, rc) + 1],
                    AFr[:, 2048 + oc * 4 + b_:2048 + oc * 4 + b_ + 1]))
    ee("vector", None, inc=[(s_ev, 1)])
    if S0 < 233:
        return _finalize(nc, kb)
    for kind in (1, 4):
        c0 = modcol(kind, 0, 0)
        ee("vector", lambda e, c0=c0: e.tensor_scalar_add(
            smod[:, c0:c0 + B * RD], smod[:, c0:c0 + B * RD], 1.0), inc=[(s_ev, 1)])
    for kind in (2, 5):
        c0 = modcol(kind, 0, 0)
        n = B * RD
        ee("scalar", lambda e, c0=c0, n=n: e.activation(
            AFr[:, 0:n], smod[:, c0:c0 + n], AF.Sign), wait=[W(s_ev)], inc=[(s_ak, 1)])
        ee("vector", lambda e, c0=c0, n=n: e.tensor_single_scalar(
            AFr[:, n:2 * n], smod[:, c0:c0 + n], -1.0, ALU.mult))
        ee("vector", lambda e, c0=c0, n=n: e.tensor_tensor(
            AFr[:, n:2 * n], smod[:, c0:c0 + n], AFr[:, n:2 * n], ALU.max))
        ee("vector", lambda e, n=n: e.tensor_scalar_max(
            AFr[:, n:2 * n], AFr[:, n:2 * n], 0.25))
        ee("vector", lambda e, c0=c0, n=n: e.tensor_tensor(
            smod[:, c0:c0 + n], AFr[:, 0:n], AFr[:, n:2 * n], ALU.mult),
            wait=[W(s_ak)], inc=[(s_ev, 1)])
    g_mod = Gate(); g_mod.set(s_ev)
    tap("modraw", AFr[:, 2048:2048 + 192])
    g_s4[0].set(s_ev)
    g_s4[1].set(s_ev)
    tap("mod", smod[:, 0:96])
    g_af_x.set_many([s_mm, s_ev, s_ak])
    g_af_s.set_many([s_mm, s_ev, s_ak])

    HB = (0, 4096)
    WQO, WKO, WVO = 8192, 16384, 24576
    KFO, VAO, QZO = 32768, 49152, 73856
    PBO = (QZO, QZO + 2048)
    OFMO = WKO
    TSBO = WKO + 4096
    RECO = WKO + 4608
    BCSO = WKO + 5120

    def stage1(b):
        for (key, off) in (("wqT", WQO), ("wkT", WKO), ("wvT", WVO)):
            wts = [W(s_mm)]
            if b == 1:
                wts = list(g_wqo_busy.w()) if key == "wqT" else [W(s_ev)]
            dsem = D_[key[0:2]]
            ee("gpsimd", lambda e, key=key, off=off: e.dma_start(
                out=ab(off, RD * D).rearrange("p (rc o) -> p rc o", rc=RD),
                in_=wT_h[key].rearrange("(rc p) o -> p rc o", p=128)),
                wait=wts, inc16=dsem)
            g_w[key] = Gate()
            g_w[key].set(dsem)
        ee("vector", lambda e: e.memset(ab(VAO, NKC * VROW), 0.0), wait=g_va.w())
        for base in (64, 65):
            ee("vector", lambda e, base=base: e.memset(
                AB[:, VAO:VAO + NKC * VROW].rearrange(
                    "p (kc pr c) -> p kc pr c", kc=NKC, pr=H // 2)[:, :, :, base:base + 1],
                1.0))
        ee("vector", None, inc=[(s_ev, 1)])
        g_vinit = Gate(); g_vinit.set(s_ev)

        for tcl in range(NTCB):
            hb = HB[tcl % 2]
            src = x_in[b, tcl * TC:(tcl + 1) * TC, :].rearrange("(tt p) f -> p tt f", p=128)
            dx = D_["x0"] if tcl % 2 == 0 else D_["x1"]
            ee("sync", lambda e, src=src: e.dma_start(
                out=af(0, 4096).rearrange("p (tt f) -> p tt f", tt=4), in_=src),
                wait=g_af_x.w(), inc16=dx)
            ld_x = (dx, dx.count)
            for tt in range(4):
                xt = af(0, 4096)[:, tt * D:(tt + 1) * D]
                sc = 8 * tt
                ee("scalar", lambda e, xt=xt, sc=sc, tt=tt: e.activation(
                    ab(QZO + tt * D, D), xt, AF.Square,
                    accum_out=stat[:, sc + 1:sc + 2]),
                    wait=[ld_x] + (g_qz.w() if tt == 0 else []),
                    inc=[(s_ak, 1)])
                ak_sq = s_ak.count
                ee("vector", lambda e, xt=xt, sc=sc: e.reduce_sum(
                    stat[:, sc:sc + 1], xt, AX),
                    wait=[ld_x]
                         + (g_mod.w() if (b, tcl, tt) == (0, 0, 0) else []))
                ee("vector", lambda e, sc=sc: e.tensor_single_scalar(
                    stat[:, sc + 2:sc + 3], stat[:, sc:sc + 1], -1.0 / D, ALU.mult))
                ee("vector", lambda e, sc=sc: e.tensor_tensor(
                    stat[:, sc + 3:sc + 4], stat[:, sc + 2:sc + 3],
                    stat[:, sc + 2:sc + 3], ALU.mult))
                ee("vector", lambda e, sc=sc: e.scalar_tensor_tensor(
                    stat[:, sc + 4:sc + 5], stat[:, sc + 1:sc + 2], 1.0 / D,
                    stat[:, sc + 3:sc + 4], ALU.mult, ALU.subtract),
                    wait=[(s_ak, ak_sq)], inc=[(s_ev, 1)])
                ee("scalar", lambda e, sc=sc: e.activation(
                    stat[:, sc + 5:sc + 6], stat[:, sc + 4:sc + 5], AF.Ln,
                    bias=consts[:, 1:2]), wait=[W(s_ev)])
                ee("scalar", lambda e, sc=sc: e.activation(
                    stat[:, sc + 6:sc + 7], stat[:, sc + 5:sc + 6], AF.Exp,
                    scale=-0.5), inc=[(s_ak, 1)])
                ak_rstd = s_ak.count
                ee("vector", lambda e, xt=xt, sc=sc, tt=tt: e.tensor_scalar(
                    ab(QZO + tt * D, D), xt,
                    stat[:, sc + 2:sc + 3], stat[:, sc + 6:sc + 7],
                    ALU.add, ALU.mult),
                    wait=[(s_ak, ak_rstd)], inc=[(s_ev, 1)])
            ev_z = s_ev.count
            for rc in range(RD):
                zsl = pzt[:, 0:512]
                ee("tensor", None,
                   wait=[(s_ev, ev_z)] + g_ptf.w() + g_pzt.w())
                for tt in range(4):
                    ee("tensor", lambda e, tt=tt, rc=rc: e.transpose(
                        ptf[:, tt * 128:(tt + 1) * 128],
                        af(0, 4096)[:, tt * D + rc * 128:tt * D + (rc + 1) * 128],
                        ident_f[:]))
                for tt in range(4):
                    ee("tensor", lambda e, tt=tt, rc=rc, zsl=zsl: e.transpose(
                        zsl[:, tt * 128:(tt + 1) * 128],
                        ab(QZO + tt * D + rc * 128, 128),
                        ident_bf[:]))
                ee("tensor", None, inc=[(s_mm, 1)])
                mm_tp = s_mm.count
                ee("scalar", lambda e, rc=rc: e.copy(
                    af(4096, 4096)[:, rc * 512:(rc + 1) * 512], ptf[:, :]),
                    wait=[(s_mm, mm_tp)] + (g_af_s.w() if rc == 0 else []))
                ee("scalar", lambda e, rc=rc, hb=hb, zsl=zsl, b=b: e.activation(
                    ab(hb, 4096)[:, rc * 512:(rc + 1) * 512], zsl, AF.Identity,
                    bias=smod[:, modcol(0, b, rc):modcol(0, b, rc) + 1],
                    scale=smod[:, modcol(1, b, rc):modcol(1, b, rc) + 1]),
                    wait=(g_h[tcl % 2].w() if rc == 0 else []), inc=[(s_ak, 1)])
                g_ptf.set(s_ak)
                g_pzt.set(s_ak)
            ev_chunk = s_ak.count
            g_af_x.set(s_ak)
            dst = xfm_sp[:, :, b * T + tcl * TC:b * T + (tcl + 1) * TC].rearrange(
                "rc p t -> p rc t")
            ee("sync", lambda e, dst=dst: e.dma_start(
                out=dst, in_=af(4096, 4096).rearrange("p (rc t) -> p rc t", rc=RD)),
                wait=[(s_ak, ev_chunk)], inc16=D_["xs"])
            g_af_s.set(D_["xs"])
            if b == 0 and tcl == 0:
                tap("h0", ab(HB[0], 4096))
                tap("xfm0", af(4096, 4096))

            for (wk, woff, bidx) in (("wqT", WQO, 0), ("wkT", WKO, 1)):
                for oc in range(RD):
                    pp = oc % 2
                    slot = s4[:, pp * 512:(pp + 1) * 512]
                    ee("tensor", None,
                       wait=[(s_ak, ev_chunk)] + g_s4[pp].w() + g_w[wk].w()
                            + (g_qz.w() if (wk == "wqT" and oc == 0) else []))
                    for ic in range(RD):
                        ee("tensor", lambda e, woff=woff, ic=ic, oc=oc, hb=hb, slot=slot: e.matmul(
                            slot,
                            ab(woff, RD * D)[:, ic * D + oc * 128:ic * D + (oc + 1) * 128],
                            ab(hb, 4096)[:, ic * 512:(ic + 1) * 512],
                            start=(ic == 0), stop=(ic == RD - 1)))
                    ee("tensor", None, inc=[(s_mm, 1)])
                    mmv = s_mm.count
                    if wk == "wqT":
                        dsl = ab(QZO, 4096)[:, oc * 512:(oc + 1) * 512]
                    else:
                        dsl = ab(KFO, RD * T)[:, oc * T + tcl * 512:oc * T + (tcl + 1) * 512]
                    ee("scalar", lambda e, dsl=dsl, slot=slot, bidx=bidx, oc=oc: e.activation(
                        dsl, slot, AF.Identity,
                        bias=sbias_c[:, oc * 4 + bidx:oc * 4 + bidx + 1]),
                        wait=[(s_mm, mmv)] + g_bias.w(), inc=[(s_ak, 1)])
                    g_s4[pp].set(s_ak)
                if wk == "wqT":
                    qdst = q_sp[b, :, :, tcl * 512:(tcl + 1) * 512].rearrange(
                        "rc p t -> p rc t")
                    ee("sync", lambda e, qdst=qdst: e.dma_start(
                        out=qdst,
                        in_=ab(QZO, 4096).rearrange("p (rc t) -> p rc t", rc=RD)),
                        wait=[W(s_ak)], inc16=D_["qs"])
                    g_qz.set(D_["qs"])
            for tt in range(4):
                kc = tcl * 4 + tt
                for half in range(2):
                    pp = 2 + half
                    slot = s4[:, pp * 512:(pp + 1) * 512]
                    ee("tensor", None,
                       wait=[(s_ak, ev_chunk)] + g_s4[pp].w() + g_w["wvT"].w()
                            + g_vinit.w() + g_brow.w())
                    for ic in range(RD):
                        ee("tensor", lambda e, ic=ic, tt=tt, half=half, hb=hb, slot=slot: e.matmul(
                            slot,
                            ab(hb, 4096)[:, ic * 512 + tt * 128:ic * 512 + (tt + 1) * 128],
                            ab(WVO, RD * D)[:, ic * D + half * 512:ic * D + (half + 1) * 512],
                            start=(ic == 0), stop=False))
                    ee("tensor", lambda e, half=half, slot=slot: e.matmul(
                        slot, ones_bf[0:1, 0:128],
                        brow[0:1, half * 512:(half + 1) * 512],
                        start=False, stop=True), inc=[(s_mm, 1)])
                    mmv = s_mm.count
                    ee("vector", None, wait=[(s_mm, mmv)])
                    for hh in range(8):
                        hcur = half * 8 + hh
                        jj = hcur // 2
                        par = hcur % 2
                        ooff = VAO + kc * VROW + jj * VBLK + (0 if par == 0 else 129)
                        icol = hcur * 64 - half * 512
                        ee("vector", lambda e, ooff=ooff, icol=icol, slot=slot: e.tensor_copy(
                            AB[:, ooff:ooff + 64], slot[:, icol:icol + 64]))
                    ee("vector", None, inc=[(s_ev, 1)])
                    g_s4[pp].set(s_ev)
        g_kfm.set(s_ak)
        g_va.set(s_ev)
        g_h[0].set(s_mm)
        g_h[1].set(s_mm)

    def stage2(b):
        ee("gpsimd", lambda e: e.dma_start(
            out=ab(WQO, RD * D).rearrange("p (rc o) -> p rc o", rc=RD),
            in_=wT_h["woT"].rearrange("(rc p) o -> p rc o", p=128)),
            wait=[W(s_mm)], inc16=D_["wo"])
        g_w["woT"] = Gate(); g_w["woT"].set(D_["wo"])

        for qt in range(NQT):
            xsrc = xfm_sp[:, :, b * T + qt * 512:b * T + (qt + 1) * 512].rearrange(
                "rc p t -> p rc t")
            ee("sync", lambda e, xsrc=xsrc: e.dma_start(
                out=af(0, 4096).rearrange("p (rc t) -> p rc t", rc=RD), in_=xsrc),
                wait=g_af_x.w(), inc16=D_["xr"])
            ld_xfm = (D_["xr"], D_["xr"].count)
            qb = HB[qt % 2]
            qsrc = q_sp[b, :, :, qt * 512:(qt + 1) * 512].rearrange("rc p t -> p rc t")
            dq = D_["q0"] if qt % 2 == 0 else D_["q1"]
            ee("sync", lambda e, qsrc=qsrc, qb=qb: e.dma_start(
                out=ab(qb, 4096).rearrange("p (rc t) -> p rc t", rc=RD), in_=qsrc),
                wait=g_h[qt % 2].w(), inc16=dq)
            ld_q = (dq, dq.count)
            kcmax = 4 * (qt + 1)
            pend_ep = None
            for hp in range(H // 2):
                pbuf = PBO[hp % 2]

                def emit_scores(kc, hp=hp, qt=qt, qb=qb, ld_q=ld_q):
                    par = kc % 2
                    sA = s4[:, par * 1024:par * 1024 + 512]
                    sB = s4[:, par * 1024 + 512:par * 1024 + 1024]
                    kcol = KFO + hp * T + kc * 128
                    qcol = qb + hp * 512
                    wts = (([ld_q] + g_kfm.w() if (hp == 0 and kc == 0) else [])
                           + g_s4[2 * par].w() + g_s4[2 * par + 1].w()
                           + (g_qz.w() if (qt == 0 and hp == 0 and kc == 0) else []))
                    ee("tensor", None, wait=wts)
                    ee("tensor", lambda e, kcol=kcol, qcol=qcol, sA=sA: e.matmul(
                        sA, AB[0:64, kcol:kcol + 128], AB[0:64, qcol:qcol + 512],
                        start=True, stop=True))
                    ee("tensor", lambda e, kcol=kcol, qcol=qcol, sB=sB: e.matmul(
                        sB, AB[64:128, kcol:kcol + 128], AB[64:128, qcol:qcol + 512],
                        start=True, stop=True, tile_position=(64, 0)),
                        inc=[(s_mm, 1)])
                    return s_mm.count

                mm_scores = {0: emit_scores(0)}
                if pend_ep is not None:
                    pend_ep()
                    pend_ep = None
                for kc in range(kcmax):
                    par = kc % 2
                    if kc + 1 < kcmax:
                        mm_scores[kc + 1] = emit_scores(kc + 1)
                    pdst = ab(pbuf + par * 1024, 1024)
                    ee("scalar", lambda e, pdst=pdst, par=par: e.activation(
                        pdst, s4[:, par * 1024:par * 1024 + 1024], AF.Exp,
                        bias=consts[:, 0:1], scale=0.125),
                        wait=[(s_mm, mm_scores[kc])]
                             + (g_consts.w() if (qt == 0 and hp == 0 and kc == 0)
                                else [])
                             + g_pslot[hp % 2][par].w(),
                        inc=[(s_ak, 1)])
                    ak_exp = s_ak.count
                    g_s4[2 * par].set(s_ak)
                    g_s4[2 * par + 1].set(s_ak)
                    avw = ([(s_ak, ak_exp)]
                           + (g_va.w() if (qt == 0 and hp == 0 and kc == 0)
                              else []))
                    if kc >= qt * 4:
                        base = qt * 512 - kc * 128
                        ee("gpsimd", lambda e, pdst=pdst, base=base: e.affine_select(
                            pdst, pdst, [[0, 2], [1, 512]], ALU.is_ge, 0.0,
                            channel_multiplier=-1, base=base),
                            wait=[(s_ak, ak_exp)], inc=[(s_gp, 1)])
                        avw.append((s_gp, s_gp.count))
                    if kc == 0:
                        avw += g_poa.w() + g_pob.w()
                    voff = VAO + kc * VROW + hp * VBLK
                    ee("tensor", None, wait=avw)
                    ee("tensor", lambda e, voff=voff, pbuf=pbuf, par=par, kc=kc, kcmax=kcmax: e.matmul(
                        poa[0:65, :], AB[:, voff:voff + 65],
                        ab(pbuf + par * 1024, 512),
                        start=(kc == 0), stop=(kc == kcmax - 1)))
                    ee("tensor", lambda e, voff=voff, pbuf=pbuf, par=par, kc=kc, kcmax=kcmax: e.matmul(
                        pob[:, :], AB[:, voff + 65:voff + 193],
                        ab(pbuf + par * 1024 + 512, 512),
                        start=(kc == 0), stop=(kc == kcmax - 1)),
                        inc=[(s_mm, 1)])
                    g_pslot[hp % 2][par].set(s_mm)
                mm_av = s_mm.count

                def make_ep(hp=hp, mm_av=mm_av):
                    def _ep():
                        ee("vector", lambda e: e.tensor_copy(
                            AB[64:65, RECO:RECO + 512], poa[64:65, :]),
                            wait=[(s_mm, mm_av)])
                        ee("vector", lambda e: e.tensor_copy(
                            AB[0:1, RECO:RECO + 512], pob[0:1, :]),
                            inc=[(s_ev, 1)])
                        ev_rec = s_ev.count
                        ee("vector", lambda e: e.tensor_copy(
                            ab(BCSO + 512, 512)[0:64, :], poa[0:64, :]),
                            wait=g_bcs.w())
                        ee("vector", lambda e: e.tensor_copy(
                            ab(BCSO + 512, 512)[64:128, :], pob[64:128, :]),
                            inc=[(s_ev, 1)])
                        g_poa.set(s_ev)
                        g_pob.set(s_ev)
                        bslot = s4[:, 1536:2048]
                        ee("tensor", None,
                           wait=[(s_ev, ev_rec)] + g_s4[3].w())
                        ee("tensor", lambda e, bslot=bslot: e.matmul(
                            bslot[0:64, :], ones_bf[64:65, 0:64],
                            AB[64:65, RECO:RECO + 512], start=True, stop=True,
                            tile_position=(64, 0)))
                        ee("tensor", lambda e, bslot=bslot: e.matmul(
                            bslot[64:128, :], ones_bf[0:1, 0:64],
                            AB[0:1, RECO:RECO + 512], start=True, stop=True,
                            tile_position=(0, 64)), inc=[(s_mm, 1)])
                        mm_bc = s_mm.count
                        ee("vector", lambda e, bslot=bslot: e.reciprocal(
                            ab(BCSO, 512), bslot), wait=[(s_mm, mm_bc)],
                            inc=[(s_ev, 1)])
                        g_s4[3].set(s_ev)
                        ee("vector", lambda e, hp=hp: e.tensor_tensor(
                            ab(OFMO, 4096)[:, hp * 512:(hp + 1) * 512],
                            ab(BCSO + 512, 512), ab(BCSO, 512), ALU.mult),
                            wait=(g_ofm.w() if hp == 0 else []),
                            inc=[(s_ev, 1)])
                        g_bcs.set(s_ev)
                    return _ep

                pend_ep = make_ep()
            pend_ep()
            pend_ep = None
            g_h[qt % 2].set(s_mm)
            ev_ofm = s_ev.count
            if b == 0 and qt == 0:
                tap("ofm0", ab(OFMO, 4096))
            for oc in range(RD):
                pp = oc % 2
                slot = s4[:, pp * 512:(pp + 1) * 512]
                ee("tensor", None,
                   wait=([(s_ev, ev_ofm)] if oc < 2 else []) + g_s4[pp].w()
                        + (g_w["woT"].w() if (qt == 0 and oc == 0) else []))
                for ic in range(RD):
                    ee("tensor", lambda e, ic=ic, oc=oc, slot=slot: e.matmul(
                        slot,
                        ab(WQO, RD * D)[:, ic * D + oc * 128:ic * D + (oc + 1) * 128],
                        ab(OFMO, 4096)[:, ic * 512:(ic + 1) * 512],
                        start=(ic == 0), stop=(ic == RD - 1)))
                ee("tensor", None, inc=[(s_mm, 1)])
                mmo = s_mm.count
                ee("scalar", lambda e, slot=slot, oc=oc: e.activation(
                    ab(TSBO, 512), slot, AF.Identity,
                    bias=sbias_c[:, oc * 4 + 3:oc * 4 + 4]),
                    wait=[(s_mm, mmo)] + g_tsb.w(), inc=[(s_ak, 1)])
                ak_t = s_ak.count
                g_s4[pp].set(s_ak)
                ee("vector", lambda e, oc=oc, b=b: e.scalar_tensor_tensor(
                    af(4096, 4096)[:, oc * 512:(oc + 1) * 512],
                    ab(TSBO, 512),
                    smod[:, modcol(2, b, oc):modcol(2, b, oc) + 1],
                    af(0, 4096)[:, oc * 512:(oc + 1) * 512],
                    ALU.mult, ALU.add),
                    wait=[(s_ak, ak_t)]
                         + ([ld_xfm] + g_af_s.w() if oc == 0 else []),
                    inc=[(s_ev, 1)])
                g_tsb.set(s_ev)
            ev_x1 = s_ev.count
            g_ofm.set(s_mm)
            g_af_x.set(s_ev)
            if qt == NQT - 1:
                g_wqo_busy.set(s_mm)
            tok0 = b * T + qt * 512
            d1 = x1_sp[:, :, tok0:tok0 + 512].rearrange("rc p t -> p rc t")
            ee("sync", lambda e, d1=d1: e.dma_start(
                out=d1, in_=af(4096, 4096).rearrange("p (rc t) -> p rc t", rc=RD)),
                wait=[(s_ev, ev_x1)], inc16=D_["x1s"])
            d2 = x1b_sp[:, :, tok0:tok0 + 512].rearrange("rc p t -> p rc t")
            ee("gpsimd", lambda e, d2=d2: e.dma_start(
                out=d2, in_=af(4096, 4096).rearrange("p (rc t) -> p rc t", rc=RD)),
                wait=[(s_ev, ev_x1)], inc16=D_["x1b"])
            g_af_s.set_many([D_["x1s"], D_["x1b"]])
            if b == 0 and qt == 0:
                tap("x1c0", af(4096, 4096))

    def dequant():
        dq_bar = [W(D_["x1s"]), W(D_["x1b"]), W(s_ev), W(s_mm)]
        ee("vector", None, wait=dq_bar)
        ee("sync", None, wait=dq_bar)
        RQO, SQO2, STR, STS = 1024, 2048, 3072, 6144 + 3072
        REOP = (12288, 24576)
        SEOP = (15360, 27648)
        OQP_D = (18432, 21504)
        NATP = (0, 3072)
        g_natp = [Gate(), Gate()]
        g_oqd = [Gate(), Gate()]
        g_rqs = [Gate(), Gate()]
        g_reop = [Gate(), Gate()]
        nat_i = 0
        oq_i = 0
        for wname, I_, O_, nat, wt, osp in (
                ("up", D, DF, wn_h["up"], wupT_h, wq_sp["up"]),
                ("g", D, DF, wn_h["g"], wgT_h, wq_sp["g"]),
                ("d", DF, D, wn_h["d"], wdT_h, wq_sp["d"])):
            nblk = I_ // 32
            nocn = O_ // 128
            nrc = I_ // 128
            for ocn in range(nocn):
                p = nat_i % 2
                nat_i += 1
                buf = NATP[p]
                dnat = D_["nat"] if p == 0 else D_["nat2"]
                ee("sync", lambda e, nat=nat, ocn=ocn, I_=I_, buf=buf: e.dma_start(
                    out=af(buf, I_), in_=nat[ocn * 128:(ocn + 1) * 128, :]),
                    wait=g_natp[p].w(), inc16=dnat)
                ld_n = (dnat, dnat.count)
                ee("vector", lambda e, I_=I_, nblk=nblk, buf=buf: e.reduce_max(
                    af(6144, nblk), af(buf, I_).rearrange("p (g e) -> p g e", e=32),
                    AX, apply_absolute_value=True), wait=[ld_n])
                ee("vector", lambda e, nblk=nblk: e.reciprocal(
                    af(6400, nblk), af(6144, nblk)))
                rqo = RQO + p * 256
                sqo = SQO2 + p * 256
                ee("vector", lambda e, nblk=nblk, rqo=rqo: e.tensor_single_scalar(
                    ab(rqo, 256)[:, 0:nblk], af(6400, nblk), 7.0, ALU.mult),
                    wait=g_rqs[p].w())
                ee("vector", lambda e, nblk=nblk, sqo=sqo: e.tensor_single_scalar(
                    ab(sqo, 256)[:, 0:nblk], af(6144, nblk), 1.0 / 7.0, ALU.mult),
                    inc=[(s_ev, 1)])
                evs = s_ev.count
                g_natp[p] = Gate()
                g_natp[p].set(s_ev)
                ee("tensor", None, wait=[(s_ev, evs)] + g_pzt.w())
                ee("tensor", lambda e, nblk=nblk, rqo=rqo: e.transpose(
                    pzt[0:nblk, 0:128], ab(rqo, 256)[:, 0:nblk], ident_bf[:]))
                ee("tensor", lambda e, nblk=nblk, sqo=sqo: e.transpose(
                    pzt[0:nblk, 512:640], ab(sqo, 256)[:, 0:nblk], ident_bf[:]),
                    inc=[(s_mm, 1)])
                g_rqs[p] = Gate()
                g_rqs[p].set(s_mm)
                ee("scalar", lambda e, ocn=ocn, nblk=nblk: e.copy(
                    AB[0:nblk, STR + ocn * 128:STR + (ocn + 1) * 128],
                    pzt[0:nblk, 0:128]), wait=[W(s_mm)])
                ee("scalar", lambda e, ocn=ocn, nblk=nblk: e.copy(
                    AB[0:nblk, STS + ocn * 128:STS + (ocn + 1) * 128],
                    pzt[0:nblk, 512:640]), inc=[(s_ak, 1)])
                g_pzt.set(s_ak)
            for rc in range(nrc):
                p = nat_i % 2
                nat_i += 1
                buf = NATP[p]
                dwt = D_["wt"] if p == 0 else D_["wt2"]
                ee("sync", lambda e, wt=wt, rc=rc, O_=O_, buf=buf: e.dma_start(
                    out=af(buf, O_), in_=wt[rc * 128:(rc + 1) * 128, :]),
                    wait=g_natp[p].w(), inc16=dwt)
                ld_w = (dwt, dwt.count)
                blk = (4 * rc) // 32 * 32
                kloc = 4 * rc - blk
                reo = REOP[p]
                seo = SEOP[p]
                first_sc = True
                for (srccol, dsto) in ((STR, reo), (STS, seo)):
                    for oh in range(O_ // 512):
                        slot = s4[:, (oh % 2) * 512:(oh % 2) * 512 + 512]
                        ee("tensor", None,
                           wait=g_s4[oh % 2].w() + [W(s_ak)] + g_idents.w())
                        ee("tensor", lambda e, srccol=srccol, oh=oh, blk=blk, kloc=kloc, slot=slot: e.matmul(
                            slot,
                            evar3[blk:blk + 32, (kloc // 4) * 128:(kloc // 4) * 128 + 128],
                            AB[blk:blk + 32, srccol + oh * 512:srccol + (oh + 1) * 512],
                            start=True, stop=True), inc=[(s_mm, 1)])
                        mve = s_mm.count
                        if oh % 2 == 0:
                            ee("vector", lambda e, dsto=dsto, oh=oh, slot=slot: e.tensor_copy(
                                ab(dsto, 3072)[:, oh * 512:(oh + 1) * 512], slot),
                                wait=[(s_mm, mve)], inc=[(s_ev, 1)])
                            g_s4[oh % 2].sc = [(s_ev, s_ev.count)]
                        else:
                            ee("scalar", lambda e, dsto=dsto, oh=oh, slot=slot: e.copy(
                                ab(dsto, 3072)[:, oh * 512:(oh + 1) * 512], slot),
                                wait=[(s_mm, mve)]
                                     + (g_reop[p].w() if first_sc else []),
                                inc=[(s_ak, 1)])
                            first_sc = False
                            g_s4[oh % 2].sc = [(s_ak, s_ak.count)]
                ev_exp = s_ev.count
                ak_exp = s_ak.count
                po = oq_i % 2
                oq_i += 1
                oqo = OQP_D[po]
                dq = D_["qw"] if po == 0 else D_["qw2"]
                for oh in range(O_ // 512):
                    ip = oh % 2
                    ee("vector", lambda e, oh=oh, ip=ip, buf=buf, reo=reo: e.tensor_tensor(
                        AI[:, ip * 512:(ip + 1) * 512],
                        af(buf, 3072)[:, oh * 512:(oh + 1) * 512],
                        ab(reo, 3072)[:, oh * 512:(oh + 1) * 512], ALU.mult),
                        wait=[ld_w]
                             + ([(s_ev, ev_exp), (s_ak, ak_exp)] if oh == 0 else [])
                             + (g_oqd[po].w() if oh == 0 else []))
                    ee("vector", lambda e, oh=oh, ip=ip, oqo=oqo, seo=seo: e.tensor_tensor(
                        ab(oqo, 3072)[:, oh * 512:(oh + 1) * 512],
                        AI[:, ip * 512:(ip + 1) * 512],
                        ab(seo, 3072)[:, oh * 512:(oh + 1) * 512], ALU.mult),
                        inc=[(s_ev, 1)])
                evq = s_ev.count
                g_natp[p] = Gate()
                g_natp[p].set(s_ev)
                g_reop[p] = Gate()
                g_reop[p].set(s_ev)
                ee("sync", lambda e, osp=osp, rc=rc, O_=O_, oqo=oqo: e.dma_start(
                    out=osp[rc, :, :], in_=ab(oqo, O_)),
                    wait=[(s_ev, evq)], inc16=dq)
                g_oqd[po] = Gate()
                g_oqd[po].set(dq)

    def allreduce_max(idx, srccols, dstcols, n):
        ee("gpsimd", lambda e, srccols=srccols, n=n, idx=idx: e.dma_start(
            out=cc_in[idx].ap()[:, 0:n], in_=AFr[0:1, srccols:srccols + n]),
            wait=[W(s_ev)], inc16=D_["cc"])
        ee("gpsimd", lambda e, idx=idx: e.collective_compute(
            "AllReduce", ALU.max, replica_groups=[list(range(8))],
            ins=[cc_in[idx].ap().opt()], outs=[cc_out[idx].ap().opt()]),
            wait=[W(D_["cc"])], inc=[(s_cc, 1)])
        ee("gpsimd", lambda e, dstcols=dstcols, n=n, idx=idx: e.dma_start(
            out=AFr[0:1, dstcols:dstcols + n], in_=cc_out[idx].ap()[:, 0:n]),
            wait=[W(s_cc)], inc16=D_["cc"])

    def finalize_max(mcol, dstcol):
        ee("tensor", None, wait=[W(s_gp), W(s_ev)] + g_ptf.w())
        ee("tensor", lambda e, mcol=mcol: e.transpose(
            ptf[0:1, 0:128], macc[:, mcol:mcol + 1], ident_f[:]),
            inc=[(s_mm, 1)])
        mmt = s_mm.count
        ee("scalar", lambda e: e.copy(
            AFr[0:1, 3900:4028], ptf[0:1, 0:128]),
            wait=[(s_mm, mmt)], inc=[(s_ak, 1)])
        ak_c = s_ak.count
        ee("vector", lambda e, dstcol=dstcol: e.reduce_max(
            AFr[0:1, dstcol:dstcol + 1], AFr[0:1, 3900:4028], AX,
            apply_absolute_value=True), wait=[(s_ak, ak_c)], inc=[(s_ev, 1)])
        g_ptf.set(s_ev)

    def scales_from(idx, maxcols, n, scal_base):
        for i in range(n):
            mc = maxcols + i
            ee("vector", lambda e, mc=mc, i=i: e.tensor_single_scalar(
                AFr[0:1, 3100 + 2 * i + 1:3100 + 2 * i + 2],
                AFr[0:1, mc:mc + 1], 1.0 / QMAX, ALU.mult),
                wait=[W(D_["cc"])])
            ee("vector", lambda e, i=i: e.reciprocal(
                AFr[0:1, 3100 + 2 * i:3100 + 2 * i + 1],
                AFr[0:1, 3100 + 2 * i + 1:3100 + 2 * i + 2]), inc=[(s_ev, 1)])
        evr = s_ev.count
        ee("tensor", None, wait=[(s_ev, evr)] + g_poa.w())
        ee("tensor", lambda e, n=n: e.matmul(
            poa[:, 0:2 * n], one_f[0:1, :], AFr[0:1, 3100:3100 + 2 * n],
            start=True, stop=True), inc=[(s_mm, 1)])
        mmb = s_mm.count
        ee("vector", lambda e, n=n, scal_base=scal_base: e.tensor_copy(
            sscal[:, scal_base:scal_base + 2 * n], poa[:, 0:2 * n]),
            wait=[(s_mm, mmb)], inc=[(s_ev, 1)])
        g_poa.set(s_ev)

    def ffn():
        BD = (0, 4096)
        SQF, Z2B, H2S = 8192, 12288, 16384
        g_bd = [Gate(), Gate()]
        g_z2b = Gate()
        for tc in range(NTC):
            b = tc // NTCB
            bd = BD[tc % 2]
            srcb = x1b_sp[:, :, tc * 512:(tc + 1) * 512].rearrange("rc p t -> p rc t")
            dba = D_["ba0"] if tc % 2 == 0 else D_["ba1"]
            ee("sync", lambda e, srcb=srcb, bd=bd: e.dma_start(
                out=ab(bd, 4096).rearrange("p (rc t) -> p rc t", rc=RD), in_=srcb),
                wait=g_bd[tc % 2].w(), inc16=dba)
            ld_b = (dba, dba.count)
            for rc in range(RD):
                ee("scalar", lambda e, bd=bd, rc=rc: e.activation(
                    ab(SQF, 4096)[:, rc * 512:(rc + 1) * 512],
                    ab(bd, 4096)[:, rc * 512:(rc + 1) * 512], AF.Square),
                    wait=([ld_b] if rc == 0 else []), inc=[(s_ak, 1)])
            ak_sqf = s_ak.count
            ee("tensor", None,
               wait=[ld_b, (s_ak, ak_sqf)] + g_poa.w() + g_pob.w())
            for rc in range(RD):
                ee("tensor", lambda e, bd=bd, rc=rc: e.matmul(
                    poa[0:1, :], ones_bf[:, 0:1],
                    ab(bd, 4096)[:, rc * 512:(rc + 1) * 512],
                    start=(rc == 0), stop=(rc == RD - 1)))
            for rc in range(RD):
                ee("tensor", lambda e, rc=rc: e.matmul(
                    pob[0:1, :], ones_bf[:, 0:1],
                    ab(SQF, 4096)[:, rc * 512:(rc + 1) * 512],
                    start=(rc == 0), stop=(rc == RD - 1)))
            ee("tensor", None, inc=[(s_mm, 1)])
            mm_st = s_mm.count
            ee("scalar", lambda e: e.copy(AFr[0:1, 2560:3072], poa[0:1, :]),
               wait=[(s_mm, mm_st)])
            ee("scalar", lambda e: e.copy(AFr[0:1, 2048:2560], pob[0:1, :]),
               inc=[(s_ak, 1)])
            ak_st = s_ak.count
            g_poa.set(s_ak)
            g_pob.set(s_ak)
            ee("vector", lambda e: e.tensor_single_scalar(
                AFr[0:1, 0:512], AFr[0:1, 2560:3072], 1.0 / D, ALU.mult),
                wait=[(s_ak, ak_st)])
            ee("vector", lambda e: e.tensor_tensor(
                AFr[0:1, 1024:1536], AFr[0:1, 0:512], AFr[0:1, 0:512], ALU.mult))
            ee("vector", lambda e: e.scalar_tensor_tensor(
                AFr[0:1, 1536:2048], AFr[0:1, 2048:2560], 1.0 / D,
                AFr[0:1, 1024:1536], ALU.mult, ALU.subtract), inc=[(s_ev, 1)])
            ee("scalar", lambda e: e.activation(
                AFr[0:1, 1536:2048], AFr[0:1, 1536:2048], AF.Ln,
                bias=consts[0:1, 1:2]), wait=[W(s_ev)])
            ee("scalar", lambda e: e.activation(
                AFr[0:1, 512:1024], AFr[0:1, 1536:2048], AF.Exp, scale=-0.5),
                inc=[(s_ak, 1)])
            ak_r = s_ak.count
            g_pob.set(s_ak)
            ee("tensor", None,
               wait=[(s_ak, ak_r)] + g_s4[0].w() + g_s4[1].w())
            ee("tensor", lambda e: e.matmul(
                s4[:, 0:512], one_f[0:1, :], AFr[0:1, 0:512], start=True, stop=True))
            ee("tensor", lambda e: e.matmul(
                s4[:, 512:1024], one_f[0:1, :], AFr[0:1, 512:1024],
                start=True, stop=True), inc=[(s_mm, 1)])
            mm_b = s_mm.count
            ee("vector", None, wait=[(s_mm, mm_b)] + g_z2b.w())
            for rc in range(RD):
                ee("vector", lambda e, bd=bd, rc=rc: e.tensor_tensor(
                    af(2048, 512), ab(bd, 4096)[:, rc * 512:(rc + 1) * 512],
                    s4[:, 0:512], ALU.subtract))
                ee("vector", lambda e, rc=rc: e.tensor_tensor(
                    ab(Z2B, 4096)[:, rc * 512:(rc + 1) * 512],
                    af(2048, 512), s4[:, 512:1024], ALU.mult))
                ee("vector", lambda e, rc=rc, b=b: e.tensor_scalar(
                    ab(H2S, 512), ab(Z2B, 4096)[:, rc * 512:(rc + 1) * 512],
                    smod[:, modcol(4, b, rc):modcol(4, b, rc) + 1],
                    smod[:, modcol(3, b, rc):modcol(3, b, rc) + 1],
                    ALU.mult, ALU.add), inc=[(s_ev, 1)])
                ee("vector", lambda e: e.reduce_max(
                    macc[:, 5:6], ab(H2S, 512), AX, apply_absolute_value=True))
                ee("vector", lambda e: e.tensor_tensor(
                    macc[:, 0:1], macc[:, 0:1], macc[:, 5:6], ALU.max),
                    inc=[(s_ev, 1)])
            ee("vector", None, inc=[(s_ev, 1)])
            evz2 = s_ev.count
            g_s4[0].set(s_ev)
            g_s4[1].set(s_ev)
            g_bd[tc % 2].set(s_ev)
            dz = z2_sp[:, :, tc * 512:(tc + 1) * 512].rearrange("rc p t -> p rc t")
            ee("sync", lambda e, dz=dz: e.dma_start(
                out=dz, in_=ab(Z2B, 4096).rearrange("p (rc t) -> p rc t", rc=RD)),
                wait=[(s_ev, evz2)], inc16=D_["zs"])
            g_z2b.set(D_["zs"])
            if tc == 0:
                tap("z2c0", ab(Z2B, 4096))
        finalize_max(0, 3000)
        allreduce_max(0, 3000, 3050, 1)
        scales_from(0, 3050, 1, 0)
        for b in range(B):
            for rc in range(RD):
                col = b * RD + rc
                ee("vector", lambda e, col=col, b=b, rc=rc: e.tensor_tensor(
                    qfold[:, col:col + 1],
                    smod[:, modcol(4, b, rc):modcol(4, b, rc) + 1],
                    sscal[:, 0:1], ALU.mult))
                ee("vector", lambda e, col=col, b=b, rc=rc: e.tensor_tensor(
                    qfold[:, 16 + col:16 + col + 1],
                    smod[:, modcol(3, b, rc):modcol(3, b, rc) + 1],
                    sscal[:, 0:1], ALU.mult), inc=[(s_ev, 1)])
        g_qf = Gate(); g_qf.set(s_ev)

        WUP, WG = 0, 24576
        Z2P = (49152, 53248)
        UB, GB = 57344, 63488
        ee("sync", lambda e: e.dma_start(
            out=ab(WUP, RD * DF).rearrange("p (rc o) -> p rc o", rc=RD),
            in_=wq_sp["up"].rearrange("rc p o -> p rc o")),
            wait=[W(s_ev), W(s_ak), W(s_mm), W(D_["qw"]), W(D_["qw2"]),
                  W(D_["zs"])], inc16=D_["wu"])
        ee("sync", lambda e: e.dma_start(
            out=ab(WG, RD * DF).rearrange("p (rc o) -> p rc o", rc=RD),
            in_=wq_sp["g"].rearrange("rc p o -> p rc o")), inc16=D_["wg"])
        ld_wug = [(D_["wu"], D_["wu"].count), (D_["wg"], D_["wg"].count)]
        bar_b = [W(s_mm), W(s_ev), W(s_ak), W(s_gp), W(D_["zs"])]
        ee("vector", None, wait=bar_b)
        ee("scalar", None, wait=bar_b)
        ee("gpsimd", None, wait=bar_b)
        g_z2p = [Gate(), Gate()]
        g_z2p[0].sc = list(bar_b)
        g_z2p[1].sc = list(bar_b)
        g_ub, g_gb = Gate(), Gate()
        def reload_b(tc):
            zp = Z2P[tc % 2]
            sz = z2_sp[:, :, tc * 512:(tc + 1) * 512].rearrange("rc p t -> p rc t")
            dzr = D_["zr0"] if tc % 2 == 0 else D_["zr1"]
            ee("sync", lambda e, sz=sz, zp=zp: e.dma_start(
                out=ab(zp, 4096).rearrange("p (rc t) -> p rc t", rc=RD), in_=sz),
                wait=g_z2p[tc % 2].w(), inc16=dzr)
            return (dzr, dzr.count)

        ldb = {0: reload_b(0)}
        for tc in range(NTC):
            b = tc // NTCB
            zp = Z2P[tc % 2]
            if tc + 1 < NTC:
                ldb[tc + 1] = reload_b(tc + 1)
            ld_z = ldb[tc]
            for rc in range(RD):
                ip = rc % 2
                ee("vector", lambda e, zp=zp, rc=rc, b=b, ip=ip: e.tensor_scalar(
                    AI[:, ip * 512:(ip + 1) * 512],
                    ab(zp, 4096)[:, rc * 512:(rc + 1) * 512],
                    qfold[:, b * RD + rc:b * RD + rc + 1],
                    qfold[:, 16 + b * RD + rc:16 + b * RD + rc + 1],
                    ALU.mult, ALU.add),
                    wait=[ld_z] + (g_qf.w() if tc == 0 and rc == 0 else []))
                ee("vector", lambda e, zp=zp, rc=rc, ip=ip: e.tensor_single_scalar(
                    ab(zp, 4096)[:, rc * 512:(rc + 1) * 512],
                    AI[:, ip * 512:(ip + 1) * 512], sscal[:, 1:2], ALU.mult),
                    inc=[(s_ev, 1)])
            ev_xq = s_ev.count
            if tc == 0:
                tap("xqc0", ab(Z2P[0], 4096))
            for (woff, obuf, mc, spd) in ((WUP, UB, 1, u_sp), (WG, GB, 2, g_sp)):
                for oc in range(RF):
                    pp = (0 if woff == WUP else 2) + oc % 2
                    slot = s4[:, pp * 512:(pp + 1) * 512]
                    ee("tensor", None,
                       wait=[(s_ev, ev_xq)] + ld_wug + g_s4[pp].w())
                    for ic in range(RD):
                        ee("tensor", lambda e, woff=woff, ic=ic, oc=oc, zp=zp, slot=slot: e.matmul(
                            slot,
                            ab(woff, RD * DF)[:, ic * DF + oc * 128:ic * DF + (oc + 1) * 128],
                            ab(zp, 4096)[:, ic * 512:(ic + 1) * 512],
                            start=(ic == 0), stop=(ic == RD - 1)))
                    ee("tensor", None, inc=[(s_mm, 1)])
                    mmv = s_mm.count
                    oslot = (obuf, oc % 12)
                    eng = "vector" if oc % 2 == 0 else "scalar"
                    gb_ = g_ub if woff == WUP else g_gb
                    wts = [(s_mm, mmv)]
                    if oc % 12 <= 1:
                        wts += gb_.w()
                    if eng == "vector":
                        ee("vector", lambda e, obuf=obuf, oc=oc, slot=slot: e.tensor_copy(
                            ab(obuf, 6144)[:, (oc % 12) * 512:(oc % 12 + 1) * 512], slot),
                            wait=wts, inc=[(s_ev, 1)])
                        done = (s_ev, s_ev.count)
                    else:
                        ee("scalar", lambda e, obuf=obuf, oc=oc, slot=slot: e.copy(
                            ab(obuf, 6144)[:, (oc % 12) * 512:(oc % 12 + 1) * 512], slot),
                            wait=wts, inc=[(s_ak, 1)])
                        done = (s_ak, s_ak.count)
                    g_s4[pp].sc = [done]
                    if oc % 12 == 11:
                        ee("vector", lambda e, obuf=obuf: e.reduce_max(
                            macc[:, 5:6], ab(obuf, 6144), AX,
                            apply_absolute_value=True), wait=[W(s_ev), W(s_ak)])
                        ee("vector", lambda e, mc=mc: e.tensor_tensor(
                            macc[:, mc:mc + 1], macc[:, mc:mc + 1], macc[:, 5:6],
                            ALU.max), inc=[(s_ev, 1)])
                    if oc % 12 == 11:
                        half = oc // 12
                        dspd = spd[half * 12:(half + 1) * 12, :, tc * 512:(tc + 1) * 512]
                        dsp = D_["us"] if woff == WUP else D_["gs"]
                        ee("sync", lambda e, dspd=dspd, obuf=obuf: e.dma_start(
                            out=dspd.rearrange("rf p t -> p rf t"),
                            in_=ab(obuf, 6144).rearrange("p (rf t) -> p rf t", rf=12)),
                            wait=[W(s_ev), W(s_ak)], inc16=dsp)
                        gb_.set(dsp)
            g_z2p[tc % 2].set(s_mm)
        finalize_max(1, 3010)
        finalize_max(2, 3011)
        allreduce_max(1, 3010, 3052, 2)
        scales_from(1, 3052, 2, 2)

        GR = (0, 12288)
        GATEP = (24576, 36864)
        G2 = 49152
        AIG = 61440
        SIGR = 63488
        g_gr = [Gate(), Gate()]
        g_g2h = [Gate(), Gate()]
        g_gatep = [Gate(), Gate()]
        g_ai = [Gate(), Gate()]
        bar_c = [W(s_mm), W(s_ev), W(s_ak), W(s_gp), W(D_["us"]), W(D_["gs"])]
        ee("sync", None, wait=bar_c)
        ee("vector", None, wait=bar_c)
        ee("scalar", None, wait=bar_c)
        ee("gpsimd", None, wait=bar_c)
        for gg in (g_gr[0], g_gr[1], g_g2h[0], g_g2h[1], g_gatep[0],
                   g_gatep[1], g_ai[0], g_ai[1]):
            gg.sc = list(bar_c)

        def reload_c(tc):
            gr = GR[tc % 2]
            sg = g_sp[:, :, tc * 512:(tc + 1) * 512].rearrange("rf p t -> p rf t")
            dgr = D_["gr0"] if tc % 2 == 0 else D_["gr1"]
            ee("sync", lambda e, sg=sg, gr=gr: e.dma_start(
                out=ab(gr, 12288).rearrange("p (rf t) -> p rf t", rf=RF), in_=sg),
                wait=g_gr[tc % 2].w(), inc16=dgr)
            return (dgr, dgr.count)

        ldc = {0: reload_c(0)}
        for tc in range(NTC):
            gr = GR[tc % 2]
            gate = GATEP[tc % 2]
            if tc + 1 < NTC:
                ldc[tc + 1] = reload_c(tc + 1)
            ld_g = ldc[tc]
            ev_h1 = None
            tok_s = {}

            def emit_stt(rf, gate=gate):
                ip = rf % 2
                sl = slice(rf * 512, (rf + 1) * 512)
                ee("vector", lambda e, gate=gate, sl=sl, ip=ip: e.scalar_tensor_tensor(
                    ab(G2, 12288)[:, sl], AI[:, ip * 512:(ip + 1) * 512],
                    1.0 / 65536.0, ab(gate, 12288)[:, sl], ALU.mult, ALU.mult),
                    wait=[(s_ak, tok_s[rf])]
                         + (g_g2h[0].w() if rf == 0 else [])
                         + (g_g2h[1].w() if rf == 12 else []),
                    inc=[(s_ev, 1)])
                g_ai[ip] = Gate()
                g_ai[ip].set(s_ev)

            for rf in range(RF):
                ip = rf % 2
                sl = slice(rf * 512, (rf + 1) * 512)
                aig = ab(AIG + ip * 1024, 1024).bitcast(I32)
                sigr = ab(SIGR + ip * 512, 512)
                ee("vector", lambda e, gr=gr, sl=sl, aig=aig: e.tensor_single_scalar(
                    aig, ab(gr, 12288)[:, sl], sscal[:, 4:5], ALU.mult),
                    wait=([ld_g] + g_gatep[tc % 2].w() if rf == 0 else []))
                ee("vector", lambda e, gate=gate, sl=sl, aig=aig: e.tensor_single_scalar(
                    ab(gate, 12288)[:, sl], aig, sscal[:, 5:6], ALU.mult),
                    inc=[(s_ev, 1)])
                tok_g = s_ev.count
                ee("scalar", lambda e, gate=gate, sl=sl, sigr=sigr: e.activation(
                    sigr, ab(gate, 12288)[:, sl], AF.Sigmoid),
                    wait=[(s_ev, tok_g)])
                ee("scalar", lambda e, sigr=sigr, ip=ip: e.activation(
                    AI[:, ip * 512:(ip + 1) * 512], sigr, AF.Copy, scale=65536.0),
                    wait=g_ai[ip].w(), inc=[(s_ak, 1)])
                tok_s[rf] = s_ak.count
                if rf > 0:
                    emit_stt(rf - 1)
                if rf == 12:
                    ev_h1 = s_ev.count
            emit_stt(RF - 1)
            g_gr[tc % 2].set(s_ev)
            g_gatep[tc % 2] = Gate()
            g_gatep[tc % 2].set(s_ak)
            ev_done = s_ev.count
            dz1 = g2_sp[0:12, :, tc * 512:(tc + 1) * 512].rearrange("rf p t -> p rf t")
            ee("sync", lambda e, dz1=dz1: e.dma_start(
                out=dz1, in_=ab(G2, 6144).rearrange("p (rf t) -> p rf t", rf=12)),
                wait=[(s_ev, ev_h1)], inc16=D_["g2s"])
            g_g2h[0] = Gate()
            g_g2h[0].set(D_["g2s"])
            dz2 = g2_sp[12:24, :, tc * 512:(tc + 1) * 512].rearrange("rf p t -> p rf t")
            ee("sync", lambda e, dz2=dz2: e.dma_start(
                out=dz2, in_=ab(G2 + 6144, 6144).rearrange("p (rf t) -> p rf t", rf=12)),
                wait=[(s_ev, ev_done)], inc16=D_["g2s2"])
            g_g2h[1] = Gate()
            g_g2h[1].set(D_["g2s2"])
            ee("vector", lambda e: e.reduce_max(
                macc[:, 5:6], ab(G2, 12288), AX, apply_absolute_value=True))
            ee("vector", lambda e: e.tensor_tensor(
                macc[:, 3:4], macc[:, 3:4], macc[:, 5:6], ALU.max),
                inc=[(s_ev, 1)])
        finalize_max(3, 3020)
        allreduce_max(2, 3020, 3055, 1)
        scales_from(2, 3055, 1, 6)

        G2R = (0, 12288)
        UR = (24576, 36864)
        GATE2, UP = 49152, 61440
        AIGD = 73728
        g_g2r = [Gate(), Gate()]
        g_ur = [Gate(), Gate()]
        g_oph = [Gate(), Gate()]
        g_upb = Gate()
        bar_d = [W(s_mm), W(s_ev), W(s_ak), W(s_gp), W(D_["g2s"]), W(D_["g2s2"])]
        ee("sync", None, wait=bar_d)
        ee("vector", None, wait=bar_d)
        ee("scalar", None, wait=bar_d)
        ee("gpsimd", None, wait=bar_d)
        for gg in (g_g2r[0], g_g2r[1], g_ur[0], g_ur[1], g_oph[0],
                   g_oph[1], g_upb):
            gg.sc = list(bar_d)
        def reload_d(tc):
            g2r, ur = G2R[tc % 2], UR[tc % 2]
            sg = g2_sp[:, :, tc * 512:(tc + 1) * 512].rearrange("rf p t -> p rf t")
            dd = D_["dd0"] if tc % 2 == 0 else D_["dd1"]
            du = D_["du0"] if tc % 2 == 0 else D_["du1"]
            ee("sync", lambda e, sg=sg, g2r=g2r: e.dma_start(
                out=ab(g2r, 12288).rearrange("p (rf t) -> p rf t", rf=RF), in_=sg),
                wait=g_g2r[tc % 2].w(), inc16=dd)
            su = u_sp[:, :, tc * 512:(tc + 1) * 512].rearrange("rf p t -> p rf t")
            ee("sync", lambda e, su=su, ur=ur: e.dma_start(
                out=ab(ur, 12288).rearrange("p (rf t) -> p rf t", rf=RF), in_=su),
                wait=g_ur[tc % 2].w(), inc16=du)
            return [(dd, dd.count), (du, du.count)]

        ldd = {0: reload_d(0)}
        for tc in range(NTC):
            g2r, ur = G2R[tc % 2], UR[tc % 2]
            if tc + 1 < NTC:
                ldd[tc + 1] = reload_d(tc + 1)
            ld_d = ldd[tc]
            tok_up = {}

            def emit_tt(rf):
                sl = slice(rf * 512, (rf + 1) * 512)
                ee("vector", lambda e, sl=sl: e.tensor_tensor(
                    ab(GATE2, 12288)[:, sl], ab(GATE2, 12288)[:, sl],
                    ab(UP, 12288)[:, sl], ALU.mult),
                    wait=[(s_ak, tok_up[rf])], inc=[(s_ev, 1)])

            for rf in range(RF):
                ip = rf % 2
                sl = slice(rf * 512, (rf + 1) * 512)
                aig = ab(AIGD + ip * 1024, 1024).bitcast(I32)
                ee("vector", lambda e, g2r=g2r, sl=sl, aig=aig: e.tensor_single_scalar(
                    aig, ab(g2r, 12288)[:, sl], sscal[:, 6:7], ALU.mult),
                    wait=((ld_d + g_oph[0].w()) if rf == 0 else [])
                         + (g_oph[1].w() if rf == 12 else []))
                ee("vector", lambda e, sl=sl, aig=aig: e.tensor_single_scalar(
                    ab(GATE2, 12288)[:, sl], aig, sscal[:, 7:8], ALU.mult))
                ee("scalar", lambda e, ur=ur, sl=sl, ip=ip: e.activation(
                    AI[:, ip * 512:(ip + 1) * 512], ab(ur, 12288)[:, sl],
                    AF.Copy, scale=sscal[:, 2:3]),
                    wait=((ld_d + g_upb.w()) if rf == 0 else []))
                ee("scalar", lambda e, sl=sl, ip=ip: e.activation(
                    ab(UP, 12288)[:, sl], AI[:, ip * 512:(ip + 1) * 512],
                    AF.Copy, scale=sscal[:, 3:4]), inc=[(s_ak, 1)])
                tok_up[rf] = s_ak.count
                if rf > 0:
                    emit_tt(rf - 1)
                if rf == 12:
                    ev_h1 = s_ev.count
            emit_tt(RF - 1)
            g_g2r[tc % 2].set(s_ev)
            g_ur[tc % 2].set(s_ak)
            g_upb = Gate()
            g_upb.set(s_ev)
            ev_done = s_ev.count
            ee("vector", lambda e: e.reduce_max(
                macc[:, 5:6], ab(GATE2, 12288), AX, apply_absolute_value=True))
            ee("vector", lambda e: e.tensor_tensor(
                macc[:, 4:5], macc[:, 4:5], macc[:, 5:6], ALU.max),
                inc=[(s_ev, 1)])
            ev_red = s_ev.count
            do1 = op_sp[0:12, :, tc * 512:(tc + 1) * 512].rearrange("rf p t -> p rf t")
            ee("sync", lambda e, do1=do1: e.dma_start(
                out=do1, in_=ab(GATE2, 6144).rearrange("p (rf t) -> p rf t", rf=12)),
                wait=[(s_ev, ev_h1)], inc16=D_["ops"])
            g_oph[0] = Gate()
            g_oph[0].sc = [(D_["ops"], D_["ops"].count), (s_ev, ev_red)]
            do2 = op_sp[12:24, :, tc * 512:(tc + 1) * 512].rearrange("rf p t -> p rf t")
            ee("sync", lambda e, do2=do2: e.dma_start(
                out=do2, in_=ab(GATE2 + 6144, 6144).rearrange("p (rf t) -> p rf t", rf=12)),
                wait=[(s_ev, ev_done)], inc16=D_["ops2"])
            g_oph[1] = Gate()
            g_oph[1].sc = [(D_["ops2"], D_["ops2"].count), (s_ev, ev_red)]
        finalize_max(4, 3030)
        allreduce_max(3, 3030, 3057, 1)
        scales_from(3, 3057, 1, 8)

        WDo = 0
        OPR = (24576, 36864)
        OQP = (49152, 61440)
        bar_e = [W(s_mm), W(s_ev), W(s_ak), W(s_gp), W(D_["ops"]), W(D_["ops2"])]
        ee("sync", None, wait=bar_e)
        ee("vector", None, wait=bar_e)
        ee("scalar", None, wait=bar_e)
        ee("sync", lambda e: e.dma_start(
            out=ab(WDo, RF * D).rearrange("p (rf o) -> p rf o", rf=RF),
            in_=wq_sp["d"].rearrange("rf p o -> p rf o")),
            wait=[W(s_ev), W(s_mm), W(s_ak), W(s_gp), W(D_["qw"]), W(D_["qw2"]),
                  W(D_["ops"]), W(D_["ops2"])], inc16=D_["wd"])
        ld_wd = (D_["wd"], D_["wd"].count)
        g_opr = [Gate(), Gate()]
        g_oqp = [Gate(), Gate()]
        g_x1r = Gate()
        g_outstage = Gate()
        def reload_e(tc):
            opr = OPR[tc % 2]
            so = op_sp[:, :, tc * 512:(tc + 1) * 512].rearrange("rf p t -> p rf t")
            dor = D_["or0"] if tc % 2 == 0 else D_["or1"]
            ee("sync", lambda e, so=so, opr=opr: e.dma_start(
                out=ab(opr, 12288).rearrange("p (rf t) -> p rf t", rf=RF), in_=so),
                wait=g_opr[tc % 2].w(), inc16=dor)
            return (dor, dor.count)

        lde = {0: reload_e(0)}
        for tc in range(NTC):
            b = tc // NTCB
            opr = OPR[tc % 2]
            if tc + 1 < NTC:
                lde[tc + 1] = reload_e(tc + 1)
            ld_o = lde[tc]
            sx1 = x1_sp[:, :, tc * 512:(tc + 1) * 512].rearrange("rc p t -> p rc t")
            ee("sync", lambda e, sx1=sx1: e.dma_start(
                out=af(0, 4096).rearrange("p (rc t) -> p rc t", rc=RD), in_=sx1),
                wait=g_x1r.w(), inc16=D_["x1r"])
            ld_x1 = (D_["x1r"], D_["x1r"].count)
            oq = OQP[tc % 2]
            for rf in range(RF):
                ip = rf % 2
                sl = slice(rf * 512, (rf + 1) * 512)
                ee("vector", lambda e, opr=opr, sl=sl, ip=ip: e.tensor_single_scalar(
                    AI[:, ip * 512:(ip + 1) * 512], ab(opr, 12288)[:, sl],
                    sscal[:, 8:9], ALU.mult),
                    wait=([ld_o] if rf == 0 else []) + (g_oqp[tc % 2].w() if rf == 0 else []))
                ee("vector", lambda e, oq=oq, sl=sl, ip=ip: e.tensor_single_scalar(
                    ab(oq, 12288)[:, sl], AI[:, ip * 512:(ip + 1) * 512],
                    sscal[:, 9:10], ALU.mult), inc=[(s_ev, 1)])
            ev_oq = s_ev.count
            g_opr[tc % 2].set(s_ev)
            for oc in range(RD):
                pp = oc % 2
                slot = s4[:, pp * 512:(pp + 1) * 512]
                ee("tensor", None,
                   wait=[(s_ev, ev_oq), ld_wd] + g_s4[pp].w())
                for ic in range(RF):
                    ee("tensor", lambda e, oq=oq, ic=ic, oc=oc, slot=slot: e.matmul(
                        slot,
                        ab(WDo, RF * D)[:, ic * D + oc * 128:ic * D + (oc + 1) * 128],
                        ab(oq, 12288)[:, ic * 512:(ic + 1) * 512],
                        start=(ic == 0), stop=(ic == RF - 1)))
                ee("tensor", None, inc=[(s_mm, 1)])
                mmd = s_mm.count
                ee("vector", lambda e, oc=oc, b=b, slot=slot: e.scalar_tensor_tensor(
                    af(0, 4096)[:, oc * 512:(oc + 1) * 512], slot,
                    smod[:, modcol(5, b, oc):modcol(5, b, oc) + 1],
                    af(0, 4096)[:, oc * 512:(oc + 1) * 512],
                    ALU.mult, ALU.add),
                    wait=[(s_mm, mmd), ld_x1], inc=[(s_ev, 1)])
                g_s4[pp].set(s_ev)
            ev_x2 = s_ev.count
            g_oqp[tc % 2] = Gate()
            g_oqp[tc % 2].set(s_mm)
            for tt in range(4):
                for half in range(2):
                    ee("tensor", None,
                       wait=[(s_ev, ev_x2)] + g_ptf.w())
                    for r4 in range(4):
                        rc = half * 4 + r4
                        ee("tensor", lambda e, tt=tt, rc=rc, r4=r4: e.transpose(
                            ptf[:, r4 * 128:(r4 + 1) * 128],
                            af(0, 4096)[:, rc * 512 + tt * 128:rc * 512 + (tt + 1) * 128],
                            ident_f[:]))
                    ee("tensor", None, inc=[(s_mm, 1)])
                    mmt = s_mm.count
                    ee("scalar", lambda e, tt=tt, half=half: e.copy(
                        af(4096, 4096)[:, tt * D + half * 512:tt * D + (half + 1) * 512],
                        ptf[:, :]),
                        wait=[(s_mm, mmt)] + (g_outstage.w() if tt == 0 and half == 0 else []),
                        inc=[(s_ak, 1)])
                    g_ptf.set(s_ak)
            ev_out = s_ak.count
            g_x1r.set(s_mm)
            dst = out_hbm[b, (tc % NTCB) * 512:(tc % NTCB + 1) * 512, :].rearrange(
                "(tt p) f -> p tt f", p=128)
            ee("sync", lambda e, dst=dst: e.dma_start(
                out=dst, in_=af(4096, 4096).rearrange("p (tt f) -> p tt f", tt=4)),
                wait=[(s_ak, ev_out)], inc16=D_["out"])
            g_outstage.set(D_["out"])

    seq = [lambda: stage1(0), lambda: stage2(0), lambda: stage1(1),
           lambda: stage2(1), dequant, ffn]
    for i, fn in enumerate(seq):
        if i < PHASES:
            fn()
    ee("sync", None, wait=[W(D_["out"]), W(D_["tap"])])

    return _finalize(nc, kb)


def _finalize(nc, kb):
    stack = ExitStack()
    for s in kb.sems:
        s.handle = stack.enter_context(nc.semaphore(s.name))
    for e in SELF_CHAIN:
        ss = Sem(f"self_{e}")
        ss.handle = stack.enter_context(nc.semaphore(ss.name))
        kb.selfsems[e] = ss
    stack.enter_context(nc.allow_low_precision(
        reason="bf16 matmul operands / bf16 softmax reciprocal are deliberate"))
    with nc.Block() as block:
        @block.sync
        def _(e):
            kb.replay("sync", e)

        @block.scalar
        def _(e):
            kb.replay("scalar", e)

        @block.vector
        def _(e):
            kb.replay("vector", e)

        @block.tensor
        def _(e):
            kb.replay("tensor", e)

        @block.gpsimd
        def _(e):
            kb.replay("gpsimd", e)
    stack.close()
    return nc



def kernel(**inputs):
    f32 = np.float32
    x = np.ascontiguousarray(inputs["L0_attn_x"], dtype=f32)
    biases = np.ascontiguousarray(np.stack([
        inputs["L0_attn_bq"], inputs["L0_attn_bk"],
        inputs["L0_attn_bv"], inputs["L0_attn_bo"]]).astype(f32))
    base = {
        "wqT": np.ascontiguousarray(inputs["L0_attn_Wq"].astype(f32).T),
        "wkT": np.ascontiguousarray(inputs["L0_attn_Wk"].astype(f32).T),
        "wvT": np.ascontiguousarray(inputs["L0_attn_Wv"].astype(f32).T),
        "woT": np.ascontiguousarray(inputs["L0_attn_Wo"].astype(f32).T),
        "biases": biases,
        "wupT": np.ascontiguousarray(inputs["L0_ffn_w_up"].astype(f32).T),
        "wgT": np.ascontiguousarray(inputs["L0_ffn_w_gate"].astype(f32).T),
        "wdT": np.ascontiguousarray(inputs["L0_ffn_w_down"].astype(f32).T),
        "wup_n": np.ascontiguousarray(inputs["L0_ffn_w_up"].astype(f32)),
        "wg_n": np.ascontiguousarray(inputs["L0_ffn_w_gate"].astype(f32)),
        "wd_n": np.ascontiguousarray(inputs["L0_ffn_w_down"].astype(f32)),
        "wmod": np.ascontiguousarray(inputs["L0_wmod"].astype(f32)),
    }
    cc = np.ascontiguousarray(inputs["L0_c"], dtype=f32)
    nc_obj = build_nc(n_dbg=max(list(DBG.values()) + [-1]) + 1)
    in_maps = []
    for core in range(8):
        m = dict(base)
        m["x"] = np.ascontiguousarray(x[2 * core:2 * core + 2])
        m["c"] = np.ascontiguousarray(cc[2 * core:2 * core + 2])
        in_maps.append(m)
    res = run_bass_kernel_spmd(nc_obj, in_maps, core_ids=list(range(8)))
    out = np.concatenate([res.results[c]["out"] for c in range(8)], axis=0)
    if DBG:
        kernel.dbg = [res.results[0][f"dbg{i}"]
                      for i in range(max(DBG.values()) + 1)]
    return out


# version 11
# speedup vs baseline: 1.0091x; 1.0091x over previous
import numpy as np
from contextlib import ExitStack

import concourse.bass as bass
import concourse.mybir as mybir
from concourse.bass_utils import run_bass_kernel_spmd

F32 = mybir.dt.float32
BF16 = mybir.dt.bfloat16
I32 = mybir.dt.int32
ALU = mybir.AluOpType
AF = mybir.ActivationFunctionType
AX = mybir.AxisListType.X

B, T, D, H, HD, DF = 2, 2048, 1024, 16, 64, 3072
NTOK = B * T
RD, RF = D // 128, DF // 128
TC = 512
NTCB = T // TC
NKC = T // 128
NQT = T // TC
NTC = NTOK // TC
QMAX = 32767.0
VBLK = 193
VROW = (H // 2) * VBLK

ENGS = ("sync", "scalar", "vector", "tensor", "gpsimd")

DBG = {}
PHASES = 99
S0 = 10**9


class Sem:
    __slots__ = ("name", "handle", "count")

    def __init__(self, name):
        self.name, self.handle, self.count = name, None, 0


SELF_CHAIN = ("vector", "scalar", "gpsimd")
SEM_OWNER = {"s_ev": "vector", "s_ak": "scalar", "s_gp": "gpsimd", "s_cc": "gpsimd"}


class KB:
    def __init__(self):
        self.streams = {e: [] for e in ENGS}
        self.sems = []
        self.selfsems = {e: None for e in SELF_CHAIN}
        self.selfcnt = {e: 0 for e in SELF_CHAIN}
        self.semmap = {}

    def sem(self, name):
        s = Sem(name)
        self.sems.append(s)
        return s

    def emit(self, eng, fn, wait=(), inc=(), inc16=None):
        chained = eng in SELF_CHAIN and inc16 is None
        if chained and fn is not None:
            self.selfcnt[eng] += 1
        for (s, n) in inc:
            old = s.count
            s.count += n
            if SEM_OWNER.get(s.name) is not None:
                own = SEM_OWNER[s.name]
                assert own == eng, (s.name, eng)
                for v in range(old + 1, s.count + 1):
                    self.semmap[(s.name, v)] = self.selfcnt[own]
        if inc16 is not None:
            inc16.count += 16
        self.streams[eng].append((fn, tuple(wait), tuple(inc), inc16))

    def xwait(self, e, s, v):
        own = SEM_OWNER.get(s.name)
        if own is not None:
            sc = self.semmap.get((s.name, v))
            if sc and sc > 0:
                e.wait_ge(self.selfsems[own].handle, sc)
        else:
            e.wait_ge(s.handle, v)

    def replay(self, eng_name, e):
        last_ins = None
        selfsem = self.selfsems.get(eng_name)
        nself = 0
        for (fn, waits, incs, inc16) in self.streams[eng_name]:
            for (s, v) in waits:
                if v > 0:
                    self.xwait(e, s, v)
            if selfsem is not None and inc16 is None and nself > 0:
                e.wait_ge(selfsem.handle, nself)
            ins = fn(e) if fn is not None else None
            if ins is not None:
                last_ins = ins
                if selfsem is not None and inc16 is None:
                    ins.then_inc(selfsem.handle, 1)
                    nself += 1
            keep = [(s, n) for (s, n) in incs if SEM_OWNER.get(s.name) is None]
            if keep or inc16 is not None:
                tgt = ins
                if tgt is None:
                    tgt = last_ins if last_ins is not None else e.nop()
                for (s, n) in keep:
                    tgt = tgt.then_inc(s.handle, n)
                if inc16 is not None:
                    tgt = tgt.then_inc(inc16.handle, 16)


class Gate:
    __slots__ = ("sc",)

    def __init__(self):
        self.sc = []

    def set(self, sem):
        self.sc = [(sem, sem.count)]

    def set_many(self, sems):
        self.sc = [(s, s.count) for s in sems]

    def w(self):
        return [(s, v) for (s, v) in self.sc if v > 0]


def build_nc(n_dbg=0):
    nc = bass.Bass("TRN2", target_bir_lowering=False)
    kb = KB()

    x_in = nc.dram_tensor("x", [B, T, D], F32, kind="ExternalInput").ap()
    out_hbm = nc.dram_tensor("out", [B, T, D], F32, kind="ExternalOutput").ap()
    wT_h = {k: nc.dram_tensor(k, [D, D], F32, kind="ExternalInput").ap()
            for k in ("wqT", "wkT", "wvT", "woT")}
    biases_h = nc.dram_tensor("biases", [4, D], F32, kind="ExternalInput").ap()
    wupT_h = nc.dram_tensor("wupT", [D, DF], F32, kind="ExternalInput").ap()
    wgT_h = nc.dram_tensor("wgT", [D, DF], F32, kind="ExternalInput").ap()
    wdT_h = nc.dram_tensor("wdT", [DF, D], F32, kind="ExternalInput").ap()
    wn_h = {"up": nc.dram_tensor("wup_n", [DF, D], F32, kind="ExternalInput").ap(),
            "g": nc.dram_tensor("wg_n", [DF, D], F32, kind="ExternalInput").ap(),
            "d": nc.dram_tensor("wd_n", [D, DF], F32, kind="ExternalInput").ap()}
    wmod_h = nc.dram_tensor("wmod", [D, 6 * D], F32, kind="ExternalInput").ap()
    c_in = nc.dram_tensor("c", [B, D], F32, kind="ExternalInput").ap()

    xfm_sp = nc.dram_tensor("xfm_sp", [RD, 128, NTOK], F32).ap()
    x1_sp = nc.dram_tensor("x1_sp", [RD, 128, NTOK], F32).ap()
    x1b_sp = nc.dram_tensor("x1b_sp", [RD, 128, NTOK], BF16).ap()
    z2_sp = nc.dram_tensor("z2_sp", [RD, 128, NTOK], BF16).ap()
    q_sp = nc.dram_tensor("q_sp", [B, RD, 128, T], BF16).ap()
    u_sp = nc.dram_tensor("u_sp", [RF, 128, NTOK], BF16).ap()
    g_sp = nc.dram_tensor("g_sp", [RF, 128, NTOK], BF16).ap()
    g2_sp = nc.dram_tensor("g2_sp", [RF, 128, NTOK], BF16).ap()
    op_sp = nc.dram_tensor("op_sp", [RF, 128, NTOK], BF16).ap()
    wq_sp = {"up": nc.dram_tensor("wupq_sp", [RD, 128, DF], BF16).ap(),
             "g": nc.dram_tensor("wgq_sp", [RD, 128, DF], BF16).ap(),
             "d": nc.dram_tensor("wdq_sp", [RF, 128, D], BF16).ap()}
    cc_in = [nc.dram_tensor(f"cc_in{i}", [1, 2 if i == 1 else 1], F32) for i in range(4)]
    cc_out = [nc.dram_tensor(f"cc_out{i}", [1, 2 if i == 1 else 1], F32) for i in range(4)]
    dbg = [nc.dram_tensor(f"dbg{i}", [128, 4096], F32, kind="ExternalOutput").ap()
           for i in range(n_dbg)]

    A = nc.alloc_sbuf_tensor
    ones_bf = A("ones_bf", [128, 128], BF16)
    one_f = A("one_f", [128, 128], F32)
    ident_f = A("ident_f", [128, 128], F32)
    ident_bf = A("ident_bf", [128, 128], BF16)
    evar3 = A("evar3", [96, RD * 128], BF16)
    consts = A("consts", [128, 4], F32)
    smod = A("smod", [128, 6 * B * RD], F32)
    sbias_c = A("sbias_c", [128, 4 * RD], F32)
    brow = A("brow", [1, D], BF16)
    sscal = A("sscal", [128, 16], F32)
    qfold = A("qfold", [128, 2 * B * RD], F32)
    stat = A("stat", [128, 32], F32)
    macc = A("macc", [128, 8], F32)
    AB = A("AB", [128, 77952], BF16)
    AFr = A("AFm", [128, 8192], F32)
    AI = A("AIm", [128, 1024], I32)

    s4 = nc.alloc_psum_tensor("s4", [128, 2048], F32)
    poa = nc.alloc_psum_tensor("poa", [128, 512], F32)
    pob = nc.alloc_psum_tensor("pob", [128, 512], F32)
    pzt = nc.alloc_psum_tensor("pzt", [128, 1024], BF16)
    ptf = nc.alloc_psum_tensor("ptf", [128, 512], F32)

    s_ld = kb.sem("s_ld")
    s_gld = kb.sem("s_gld")
    D_ = {name: kb.sem("d_" + name) for name in (
        "tap", "br", "bias", "c", "wm", "wq", "wk", "wv", "wo",
        "x0", "x1", "xs", "qs", "xr", "q0", "q1", "x1s", "x1b",
        "nat", "wt", "qw", "cc", "ba0", "ba1", "zs", "wu", "wg",
        "zr0", "zr1", "us", "gs", "gr0", "gr1", "g2s", "g2s2", "dd0", "dd1",
        "du0", "du1", "ops", "ops2", "wd", "or0", "or1", "x1r", "out",
        "nat2", "wt2", "qw2")}
    s_mm = kb.sem("s_mm")
    s_ev = kb.sem("s_ev")
    s_ak = kb.sem("s_ak")
    s_gp = kb.sem("s_gp")
    s_cc = kb.sem("s_cc")

    def W(s):
        return (s, s.count)

    ee = kb.emit

    def ab(o, n):
        return AB[:, o:o + n]

    def af(o, n):
        return AFr[:, o:o + n]

    def modcol(kind, b, rc):
        return kind * (B * RD) + b * RD + rc

    def tap(name, src_ap):
        if name not in DBG:
            return
        i = DBG[name]
        n = src_ap.shape[-1]
        ee("gpsimd", lambda e, i=i, n=n, src_ap=src_ap: e.dma_start(
            out=dbg[i][:, 0:n], in_=src_ap),
            wait=[W(s_mm), W(s_ak), W(s_ev), W(s_gp)], inc16=D_["tap"])
        ee("gpsimd", None, wait=[W(D_["tap"])])

    g_s4 = [Gate() for _ in range(4)]
    g_poa, g_pob, g_pzt, g_ptf = Gate(), Gate(), Gate(), Gate()
    g_af_x, g_af_s = Gate(), Gate()
    g_h = [Gate(), Gate()]
    g_qz = Gate()
    g_w = {k: Gate() for k in ("wqT", "wkT", "wvT", "woT")}
    g_wqo_busy = Gate()
    g_kfm, g_va = Gate(), Gate()
    g_pslot = [[Gate(), Gate()], [Gate(), Gate()]]
    g_ofm, g_tsb, g_bcs = Gate(), Gate(), Gate()

    ee("vector", lambda e: e.memset(ones_bf[:], 1.0))
    ee("vector", lambda e: e.memset(one_f[:], 1.0))
    ee("vector", lambda e: e.memset(consts[:, 0:1], -50.0))
    ee("vector", lambda e: e.memset(consts[:, 1:2], 1e-5))
    ee("vector", lambda e: e.memset(macc[:], 0.0), inc=[(s_ev, 1)])
    g_consts = Gate(); g_consts.set(s_ev)
    ee("gpsimd", lambda e: e.memset(ident_f[:], 1.0), inc=[(s_gp, 1)])
    ee("gpsimd", lambda e: e.affine_select(ident_f[:], ident_f[:], [[1, 128]],
                                           ALU.is_equal, 0.0, channel_multiplier=-1),
       wait=[W(s_gp)])
    ee("gpsimd", lambda e: e.memset(ident_bf[:], 1.0), inc=[(s_gp, 1)])
    ee("gpsimd", lambda e: e.affine_select(ident_bf[:], ident_bf[:], [[1, 128]],
                                           ALU.is_equal, 0.0, channel_multiplier=-1),
       wait=[W(s_gp)])
    ee("gpsimd", lambda e: e.memset(evar3[:], 1.0), inc=[(s_gp, 1)])
    for blk in range(3):
        ee("gpsimd", lambda e, blk=blk: e.affine_select(
            evar3[32 * blk:32 * (blk + 1), :], evar3[32 * blk:32 * (blk + 1), :],
            [[-4, RD], [-1, 4], [0, 32]], ALU.is_equal, 0.0, channel_multiplier=1),
            wait=[W(s_gp)])
    ee("gpsimd", None, inc=[(s_gp, 1)])
    g_idents = Gate(); g_idents.set(s_gp)

    if S0 < 1:
        return _finalize(nc, kb)
    ee("gpsimd", lambda e: e.dma_start(out=brow[:], in_=biases_h[2:3, :]), inc16=D_["br"])
    g_brow = Gate(); g_brow.set(D_["br"])
    ee("sync", lambda e: e.dma_start(out=AFr[0:4, 0:D], in_=biases_h), inc16=D_["bias"])
    ee("tensor", None, wait=[W(D_["bias"])] + g_idents.w())
    for rc in range(RD):
        ee("tensor", lambda e, rc=rc: e.transpose(
            ptf[:, rc * 4:(rc + 1) * 4],
            AFr[0:4, rc * 128:(rc + 1) * 128], ident_f[0:4, 0:4]))
    ee("tensor", None, inc=[(s_mm, 1)])
    ee("scalar", lambda e: e.copy(sbias_c[:], ptf[:, 0:4 * RD]),
       wait=[W(s_mm)], inc=[(s_ak, 1)])
    g_bias = Gate(); g_bias.set(s_ak)

    if S0 < 2:
        return _finalize(nc, kb)
    ee("vector", lambda e: e.memset(AFr[0:4, D:2 * D], 0.0),
       wait=g_bias.w(), inc=[(s_ev, 1)])
    ee("sync", lambda e: e.dma_start(out=AFr[0:B, D:2 * D], in_=c_in),
       wait=[W(s_ev)], inc16=D_["c"])
    ee("scalar", lambda e: e.activation(AFr[0:B, D:2 * D], AFr[0:B, D:2 * D], AF.Silu),
       wait=[W(D_["c"])], inc=[(s_ak, 1)])
    CFM = 60000
    g_cfm = Gate()
    if S0 < 21:
        return _finalize(nc, kb)
    ee("tensor", None, wait=[W(s_ak)] + g_bias.w())
    for rc in range(RD):
        ee("tensor", lambda e, rc=rc: e.transpose(
            ptf[:, 32 + rc * 4:32 + (rc + 1) * 4],
            AFr[0:4, D + rc * 128:D + (rc + 1) * 128], ident_f[0:4, 0:4]))
    ee("tensor", None, inc=[(s_mm, 1)])
    if S0 < 22:
        return _finalize(nc, kb)
    ee("scalar", lambda e: e.copy(ab(CFM, 4 * RD), ptf[:, 32:32 + 4 * RD]),
       wait=[W(s_mm)], inc=[(s_ak, 1)])
    g_cfm.set(s_ak)
    tap("cfm", ab(CFM, 32))
    tap("csilu", AFr[:, D:2 * D])
    tap("sbias", sbias_c[:, 0:32])

    if S0 < 23:
        return _finalize(nc, kb)
    for ic in range(RD):
        ee("gpsimd", lambda e, ic=ic: e.dma_start(
            out=ab(ic * 6144, 6144), in_=wmod_h[ic * 128:(ic + 1) * 128, :]),
            inc16=D_["wm"])
    gld_wmod = D_["wm"].count
    ee("tensor", None, wait=[(D_["wm"], gld_wmod)] + g_cfm.w())
    if S0 >= 231:
        for oc in range(6 * RD):
            for ic in range(RD):
                ee("tensor", lambda e, ic=ic, oc=oc: e.matmul(
                    s4[:, oc * 4:(oc + 1) * 4],
                    ab(ic * 6144, 6144)[:, oc * 128:(oc + 1) * 128],
                    ab(CFM, 4 * RD)[:, ic * 4:(ic + 1) * 4],
                    start=(ic == 0), stop=(ic == RD - 1)))
    ee("tensor", None, inc=[(s_mm, 1)])
    if S0 < 232:
        return _finalize(nc, kb)
    ee("scalar", lambda e: e.copy(AFr[:, 2048:2048 + 192], s4[:, 0:192]),
       wait=[W(s_mm)], inc=[(s_ak, 1)])
    if S0 < 2315:
        return _finalize(nc, kb)
    ee("vector", None, wait=[W(s_ak)])
    for kind in range(6):
        for rc in range(RD):
            oc = kind * RD + rc
            for b_ in range(B):
                ee("vector", lambda e, kind=kind, rc=rc, oc=oc, b_=b_: e.tensor_copy(
                    smod[:, modcol(kind, b_, rc):modcol(kind, b_, rc) + 1],
                    AFr[:, 2048 + oc * 4 + b_:2048 + oc * 4 + b_ + 1]))
    ee("vector", None, inc=[(s_ev, 1)])
    if S0 < 233:
        return _finalize(nc, kb)
    for kind in (1, 4):
        c0 = modcol(kind, 0, 0)
        ee("vector", lambda e, c0=c0: e.tensor_scalar_add(
            smod[:, c0:c0 + B * RD], smod[:, c0:c0 + B * RD], 1.0), inc=[(s_ev, 1)])
    for kind in (2, 5):
        c0 = modcol(kind, 0, 0)
        n = B * RD
        ee("scalar", lambda e, c0=c0, n=n: e.activation(
            AFr[:, 0:n], smod[:, c0:c0 + n], AF.Sign), wait=[W(s_ev)], inc=[(s_ak, 1)])
        ee("vector", lambda e, c0=c0, n=n: e.tensor_single_scalar(
            AFr[:, n:2 * n], smod[:, c0:c0 + n], -1.0, ALU.mult))
        ee("vector", lambda e, c0=c0, n=n: e.tensor_tensor(
            AFr[:, n:2 * n], smod[:, c0:c0 + n], AFr[:, n:2 * n], ALU.max))
        ee("vector", lambda e, n=n: e.tensor_scalar_max(
            AFr[:, n:2 * n], AFr[:, n:2 * n], 0.25))
        ee("vector", lambda e, c0=c0, n=n: e.tensor_tensor(
            smod[:, c0:c0 + n], AFr[:, 0:n], AFr[:, n:2 * n], ALU.mult),
            wait=[W(s_ak)], inc=[(s_ev, 1)])
    g_mod = Gate(); g_mod.set(s_ev)
    tap("modraw", AFr[:, 2048:2048 + 192])
    g_s4[0].set(s_ev)
    g_s4[1].set(s_ev)
    tap("mod", smod[:, 0:96])
    g_af_x.set_many([s_mm, s_ev, s_ak])
    g_af_s.set_many([s_mm, s_ev, s_ak])

    HB = (0, 4096)
    WQO, WKO, WVO = 8192, 16384, 24576
    KFO, VAO, QZO = 32768, 49152, 73856
    PBO = (QZO, QZO + 2048)
    OFMO = WKO
    TSBO = WKO + 4096
    RECO = WKO + 4608
    BCSO = WKO + 5120

    def stage1(b):
        for (key, off) in (("wqT", WQO), ("wkT", WKO), ("wvT", WVO)):
            wts = [W(s_mm)]
            if b == 1:
                wts = list(g_wqo_busy.w()) if key == "wqT" else [W(s_ev)]
            dsem = D_[key[0:2]]
            ee("gpsimd", lambda e, key=key, off=off: e.dma_start(
                out=ab(off, RD * D).rearrange("p (rc o) -> p rc o", rc=RD),
                in_=wT_h[key].rearrange("(rc p) o -> p rc o", p=128)),
                wait=wts, inc16=dsem)
            g_w[key] = Gate()
            g_w[key].set(dsem)
        ee("vector", lambda e: e.memset(ab(VAO, NKC * VROW), 0.0), wait=g_va.w())
        for base in (64, 65):
            ee("vector", lambda e, base=base: e.memset(
                AB[:, VAO:VAO + NKC * VROW].rearrange(
                    "p (kc pr c) -> p kc pr c", kc=NKC, pr=H // 2)[:, :, :, base:base + 1],
                1.0))
        ee("vector", None, inc=[(s_ev, 1)])
        g_vinit = Gate(); g_vinit.set(s_ev)

        for tcl in range(NTCB):
            hb = HB[tcl % 2]
            src = x_in[b, tcl * TC:(tcl + 1) * TC, :].rearrange("(tt p) f -> p tt f", p=128)
            dx = D_["x0"] if tcl % 2 == 0 else D_["x1"]
            ee("sync", lambda e, src=src: e.dma_start(
                out=af(0, 4096).rearrange("p (tt f) -> p tt f", tt=4), in_=src),
                wait=g_af_x.w(), inc16=dx)
            ld_x = (dx, dx.count)
            for tt in range(4):
                xt = af(0, 4096)[:, tt * D:(tt + 1) * D]
                sc = 8 * tt
                ee("scalar", lambda e, xt=xt, sc=sc, tt=tt: e.activation(
                    ab(QZO + tt * D, D), xt, AF.Square,
                    accum_out=stat[:, sc + 1:sc + 2]),
                    wait=[ld_x] + (g_qz.w() if tt == 0 else []),
                    inc=[(s_ak, 1)])
                ak_sq = s_ak.count
                ee("vector", lambda e, xt=xt, sc=sc: e.reduce_sum(
                    stat[:, sc:sc + 1], xt, AX),
                    wait=[ld_x]
                         + (g_mod.w() if (b, tcl, tt) == (0, 0, 0) else []))
                ee("vector", lambda e, sc=sc: e.tensor_single_scalar(
                    stat[:, sc + 2:sc + 3], stat[:, sc:sc + 1], -1.0 / D, ALU.mult))
                ee("vector", lambda e, sc=sc: e.tensor_tensor(
                    stat[:, sc + 3:sc + 4], stat[:, sc + 2:sc + 3],
                    stat[:, sc + 2:sc + 3], ALU.mult))
                ee("vector", lambda e, sc=sc: e.scalar_tensor_tensor(
                    stat[:, sc + 4:sc + 5], stat[:, sc + 1:sc + 2], 1.0 / D,
                    stat[:, sc + 3:sc + 4], ALU.mult, ALU.subtract),
                    wait=[(s_ak, ak_sq)], inc=[(s_ev, 1)])
                ee("scalar", lambda e, sc=sc: e.activation(
                    stat[:, sc + 5:sc + 6], stat[:, sc + 4:sc + 5], AF.Ln,
                    bias=consts[:, 1:2]), wait=[W(s_ev)])
                ee("scalar", lambda e, sc=sc: e.activation(
                    stat[:, sc + 6:sc + 7], stat[:, sc + 5:sc + 6], AF.Exp,
                    scale=-0.5), inc=[(s_ak, 1)])
                ak_rstd = s_ak.count
                ee("vector", lambda e, xt=xt, sc=sc, tt=tt: e.tensor_scalar(
                    ab(QZO + tt * D, D), xt,
                    stat[:, sc + 2:sc + 3], stat[:, sc + 6:sc + 7],
                    ALU.add, ALU.mult),
                    wait=[(s_ak, ak_rstd)], inc=[(s_ev, 1)])
            ev_z = s_ev.count
            for rc in range(RD):
                zsl = pzt[:, 0:512]
                ee("tensor", None,
                   wait=[(s_ev, ev_z)] + g_ptf.w() + g_pzt.w())
                for tt in range(4):
                    ee("tensor", lambda e, tt=tt, rc=rc: e.transpose(
                        ptf[:, tt * 128:(tt + 1) * 128],
                        af(0, 4096)[:, tt * D + rc * 128:tt * D + (rc + 1) * 128],
                        ident_f[:]))
                for tt in range(4):
                    ee("tensor", lambda e, tt=tt, rc=rc, zsl=zsl: e.transpose(
                        zsl[:, tt * 128:(tt + 1) * 128],
                        ab(QZO + tt * D + rc * 128, 128),
                        ident_bf[:]))
                ee("tensor", None, inc=[(s_mm, 1)])
                mm_tp = s_mm.count
                ee("scalar", lambda e, rc=rc: e.copy(
                    af(4096, 4096)[:, rc * 512:(rc + 1) * 512], ptf[:, :]),
                    wait=[(s_mm, mm_tp)] + (g_af_s.w() if rc == 0 else []))
                ee("scalar", lambda e, rc=rc, hb=hb, zsl=zsl, b=b: e.activation(
                    ab(hb, 4096)[:, rc * 512:(rc + 1) * 512], zsl, AF.Identity,
                    bias=smod[:, modcol(0, b, rc):modcol(0, b, rc) + 1],
                    scale=smod[:, modcol(1, b, rc):modcol(1, b, rc) + 1]),
                    wait=(g_h[tcl % 2].w() if rc == 0 else []), inc=[(s_ak, 1)])
                g_ptf.set(s_ak)
                g_pzt.set(s_ak)
            ev_chunk = s_ak.count
            g_af_x.set(s_ak)
            dst = xfm_sp[:, :, b * T + tcl * TC:b * T + (tcl + 1) * TC].rearrange(
                "rc p t -> p rc t")
            ee("sync", lambda e, dst=dst: e.dma_start(
                out=dst, in_=af(4096, 4096).rearrange("p (rc t) -> p rc t", rc=RD)),
                wait=[(s_ak, ev_chunk)], inc16=D_["xs"])
            g_af_s.set(D_["xs"])
            if b == 0 and tcl == 0:
                tap("h0", ab(HB[0], 4096))
                tap("xfm0", af(4096, 4096))

            for (wk, woff, bidx) in (("wqT", WQO, 0), ("wkT", WKO, 1)):
                for oc in range(RD):
                    pp = oc % 2
                    slot = s4[:, pp * 512:(pp + 1) * 512]
                    ee("tensor", None,
                       wait=[(s_ak, ev_chunk)] + g_s4[pp].w() + g_w[wk].w()
                            + (g_qz.w() if (wk == "wqT" and oc == 0) else []))
                    for ic in range(RD):
                        ee("tensor", lambda e, woff=woff, ic=ic, oc=oc, hb=hb, slot=slot: e.matmul(
                            slot,
                            ab(woff, RD * D)[:, ic * D + oc * 128:ic * D + (oc + 1) * 128],
                            ab(hb, 4096)[:, ic * 512:(ic + 1) * 512],
                            start=(ic == 0), stop=(ic == RD - 1)))
                    ee("tensor", None, inc=[(s_mm, 1)])
                    mmv = s_mm.count
                    if wk == "wqT":
                        dsl = ab(QZO, 4096)[:, oc * 512:(oc + 1) * 512]
                    else:
                        dsl = ab(KFO, RD * T)[:, oc * T + tcl * 512:oc * T + (tcl + 1) * 512]
                    ee("scalar", lambda e, dsl=dsl, slot=slot, bidx=bidx, oc=oc: e.activation(
                        dsl, slot, AF.Identity,
                        bias=sbias_c[:, oc * 4 + bidx:oc * 4 + bidx + 1]),
                        wait=[(s_mm, mmv)] + g_bias.w(), inc=[(s_ak, 1)])
                    g_s4[pp].set(s_ak)
                if wk == "wqT":
                    qdst = q_sp[b, :, :, tcl * 512:(tcl + 1) * 512].rearrange(
                        "rc p t -> p rc t")
                    ee("sync", lambda e, qdst=qdst: e.dma_start(
                        out=qdst,
                        in_=ab(QZO, 4096).rearrange("p (rc t) -> p rc t", rc=RD)),
                        wait=[W(s_ak)], inc16=D_["qs"])
                    g_qz.set(D_["qs"])
            for tt in range(4):
                kc = tcl * 4 + tt
                for half in range(2):
                    pp = 2 + half
                    slot = s4[:, pp * 512:(pp + 1) * 512]
                    ee("tensor", None,
                       wait=[(s_ak, ev_chunk)] + g_s4[pp].w() + g_w["wvT"].w()
                            + g_vinit.w() + g_brow.w())
                    for ic in range(RD):
                        ee("tensor", lambda e, ic=ic, tt=tt, half=half, hb=hb, slot=slot: e.matmul(
                            slot,
                            ab(hb, 4096)[:, ic * 512 + tt * 128:ic * 512 + (tt + 1) * 128],
                            ab(WVO, RD * D)[:, ic * D + half * 512:ic * D + (half + 1) * 512],
                            start=(ic == 0), stop=False))
                    ee("tensor", lambda e, half=half, slot=slot: e.matmul(
                        slot, ones_bf[0:1, 0:128],
                        brow[0:1, half * 512:(half + 1) * 512],
                        start=False, stop=True), inc=[(s_mm, 1)])
                    mmv = s_mm.count
                    ee("vector", None, wait=[(s_mm, mmv)])
                    for hh in range(8):
                        hcur = half * 8 + hh
                        jj = hcur // 2
                        par = hcur % 2
                        ooff = VAO + kc * VROW + jj * VBLK + (0 if par == 0 else 129)
                        icol = hcur * 64 - half * 512
                        ee("vector", lambda e, ooff=ooff, icol=icol, slot=slot: e.tensor_copy(
                            AB[:, ooff:ooff + 64], slot[:, icol:icol + 64]))
                    ee("vector", None, inc=[(s_ev, 1)])
                    g_s4[pp].set(s_ev)
        g_kfm.set(s_ak)
        g_va.set(s_ev)
        g_h[0].set(s_mm)
        g_h[1].set(s_mm)

    def stage2(b):
        ee("gpsimd", lambda e: e.dma_start(
            out=ab(WQO, RD * D).rearrange("p (rc o) -> p rc o", rc=RD),
            in_=wT_h["woT"].rearrange("(rc p) o -> p rc o", p=128)),
            wait=[W(s_mm)], inc16=D_["wo"])
        g_w["woT"] = Gate(); g_w["woT"].set(D_["wo"])

        for qt in range(NQT):
            xsrc = xfm_sp[:, :, b * T + qt * 512:b * T + (qt + 1) * 512].rearrange(
                "rc p t -> p rc t")
            ee("sync", lambda e, xsrc=xsrc: e.dma_start(
                out=af(0, 4096).rearrange("p (rc t) -> p rc t", rc=RD), in_=xsrc),
                wait=g_af_x.w(), inc16=D_["xr"])
            ld_xfm = (D_["xr"], D_["xr"].count)
            qb = HB[qt % 2]
            qsrc = q_sp[b, :, :, qt * 512:(qt + 1) * 512].rearrange("rc p t -> p rc t")
            dq = D_["q0"] if qt % 2 == 0 else D_["q1"]
            ee("sync", lambda e, qsrc=qsrc, qb=qb: e.dma_start(
                out=ab(qb, 4096).rearrange("p (rc t) -> p rc t", rc=RD), in_=qsrc),
                wait=g_h[qt % 2].w(), inc16=dq)
            ld_q = (dq, dq.count)
            kcmax = 4 * (qt + 1)
            pend_ep = None
            for hp in range(H // 2):
                pbuf = PBO[hp % 2]

                def emit_scores(kc, hp=hp, qt=qt, qb=qb, ld_q=ld_q):
                    par = kc % 2
                    sA = s4[:, par * 1024:par * 1024 + 512]
                    sB = s4[:, par * 1024 + 512:par * 1024 + 1024]
                    kcol = KFO + hp * T + kc * 128
                    qcol = qb + hp * 512
                    wts = (([ld_q] + g_kfm.w() if (hp == 0 and kc == 0) else [])
                           + g_s4[2 * par].w() + g_s4[2 * par + 1].w()
                           + (g_qz.w() if (qt == 0 and hp == 0 and kc == 0) else []))
                    ee("tensor", None, wait=wts)
                    ee("tensor", lambda e, kcol=kcol, qcol=qcol, sA=sA: e.matmul(
                        sA, AB[0:64, kcol:kcol + 128], AB[0:64, qcol:qcol + 512],
                        start=True, stop=True))
                    ee("tensor", lambda e, kcol=kcol, qcol=qcol, sB=sB: e.matmul(
                        sB, AB[64:128, kcol:kcol + 128], AB[64:128, qcol:qcol + 512],
                        start=True, stop=True, tile_position=(64, 0)),
                        inc=[(s_mm, 1)])
                    return s_mm.count

                mm_scores = {0: emit_scores(0)}
                if pend_ep is not None:
                    pend_ep()
                    pend_ep = None
                for kc in range(kcmax):
                    par = kc % 2
                    if kc + 1 < kcmax:
                        mm_scores[kc + 1] = emit_scores(kc + 1)
                    pdst = ab(pbuf + par * 1024, 1024)
                    ee("scalar", lambda e, pdst=pdst, par=par: e.activation(
                        pdst, s4[:, par * 1024:par * 1024 + 1024], AF.Exp,
                        bias=consts[:, 0:1], scale=0.125),
                        wait=[(s_mm, mm_scores[kc])]
                             + (g_consts.w() if (qt == 0 and hp == 0 and kc == 0)
                                else [])
                             + g_pslot[hp % 2][par].w(),
                        inc=[(s_ak, 1)])
                    ak_exp = s_ak.count
                    g_s4[2 * par].set(s_ak)
                    g_s4[2 * par + 1].set(s_ak)
                    avw = ([(s_ak, ak_exp)]
                           + (g_va.w() if (qt == 0 and hp == 0 and kc == 0)
                              else []))
                    if kc >= qt * 4:
                        base = qt * 512 - kc * 128
                        ee("gpsimd", lambda e, pdst=pdst, base=base: e.affine_select(
                            pdst, pdst, [[0, 2], [1, 512]], ALU.is_ge, 0.0,
                            channel_multiplier=-1, base=base),
                            wait=[(s_ak, ak_exp)], inc=[(s_gp, 1)])
                        avw.append((s_gp, s_gp.count))
                    if kc == 0:
                        avw += g_poa.w() + g_pob.w()
                    voff = VAO + kc * VROW + hp * VBLK
                    ee("tensor", None, wait=avw)
                    ee("tensor", lambda e, voff=voff, pbuf=pbuf, par=par, kc=kc, kcmax=kcmax: e.matmul(
                        poa[0:65, :], AB[:, voff:voff + 65],
                        ab(pbuf + par * 1024, 512),
                        start=(kc == 0), stop=(kc == kcmax - 1)))
                    ee("tensor", lambda e, voff=voff, pbuf=pbuf, par=par, kc=kc, kcmax=kcmax: e.matmul(
                        pob[:, :], AB[:, voff + 65:voff + 193],
                        ab(pbuf + par * 1024 + 512, 512),
                        start=(kc == 0), stop=(kc == kcmax - 1)),
                        inc=[(s_mm, 1)])
                    g_pslot[hp % 2][par].set(s_mm)
                mm_av = s_mm.count

                def make_ep(hp=hp, mm_av=mm_av):
                    def _ep():
                        ee("scalar", lambda e: e.copy(
                            AB[64:65, RECO:RECO + 512], poa[64:65, :]),
                            wait=[(s_mm, mm_av)])
                        ee("scalar", lambda e: e.copy(
                            AB[0:1, RECO:RECO + 512], pob[0:1, :]),
                            inc=[(s_ak, 1)])
                        ak_rec = s_ak.count
                        bslot = s4[:, 1536:2048]
                        ee("tensor", None,
                           wait=[(s_ak, ak_rec)] + g_s4[3].w())
                        ee("tensor", lambda e, bslot=bslot: e.matmul(
                            bslot[0:64, :], ones_bf[64:65, 0:64],
                            AB[64:65, RECO:RECO + 512], start=True, stop=True,
                            tile_position=(64, 0)))
                        ee("tensor", lambda e, bslot=bslot: e.matmul(
                            bslot[64:128, :], ones_bf[0:1, 0:64],
                            AB[0:1, RECO:RECO + 512], start=True, stop=True,
                            tile_position=(0, 64)), inc=[(s_mm, 1)])
                        mm_bc = s_mm.count
                        ee("vector", lambda e, bslot=bslot: e.reciprocal(
                            ab(BCSO, 512), bslot), wait=[(s_mm, mm_bc)],
                            inc=[(s_ev, 1)])
                        ee("scalar", lambda e: e.copy(
                            ab(BCSO + 512, 512)[0:64, :], poa[0:64, :]),
                            wait=g_bcs.w())
                        ee("scalar", lambda e: e.copy(
                            ab(BCSO + 512, 512)[64:128, :], pob[64:128, :]),
                            inc=[(s_ak, 1)])
                        ak_oc = s_ak.count
                        g_poa.set(s_ak)
                        g_pob.set(s_ak)
                        g_s4[3].set(s_ev)
                        ee("vector", lambda e, hp=hp: e.tensor_tensor(
                            ab(OFMO, 4096)[:, hp * 512:(hp + 1) * 512],
                            ab(BCSO + 512, 512), ab(BCSO, 512), ALU.mult),
                            wait=[(s_ak, ak_oc)] + (g_ofm.w() if hp == 0 else []),
                            inc=[(s_ev, 1)])
                        g_bcs.set(s_ev)
                    return _ep

                pend_ep = make_ep()
            pend_ep()
            pend_ep = None
            g_h[qt % 2].set(s_mm)
            ev_ofm = s_ev.count
            if b == 0 and qt == 0:
                tap("ofm0", ab(OFMO, 4096))
            for oc in range(RD):
                pp = oc % 2
                slot = s4[:, pp * 512:(pp + 1) * 512]
                ee("tensor", None,
                   wait=([(s_ev, ev_ofm)] if oc < 2 else []) + g_s4[pp].w()
                        + (g_w["woT"].w() if (qt == 0 and oc == 0) else []))
                for ic in range(RD):
                    ee("tensor", lambda e, ic=ic, oc=oc, slot=slot: e.matmul(
                        slot,
                        ab(WQO, RD * D)[:, ic * D + oc * 128:ic * D + (oc + 1) * 128],
                        ab(OFMO, 4096)[:, ic * 512:(ic + 1) * 512],
                        start=(ic == 0), stop=(ic == RD - 1)))
                ee("tensor", None, inc=[(s_mm, 1)])
                mmo = s_mm.count
                ee("scalar", lambda e, slot=slot, oc=oc: e.activation(
                    ab(TSBO, 512), slot, AF.Identity,
                    bias=sbias_c[:, oc * 4 + 3:oc * 4 + 4]),
                    wait=[(s_mm, mmo)] + g_tsb.w(), inc=[(s_ak, 1)])
                ak_t = s_ak.count
                g_s4[pp].set(s_ak)
                ee("vector", lambda e, oc=oc, b=b: e.scalar_tensor_tensor(
                    af(4096, 4096)[:, oc * 512:(oc + 1) * 512],
                    ab(TSBO, 512),
                    smod[:, modcol(2, b, oc):modcol(2, b, oc) + 1],
                    af(0, 4096)[:, oc * 512:(oc + 1) * 512],
                    ALU.mult, ALU.add),
                    wait=[(s_ak, ak_t)]
                         + ([ld_xfm] + g_af_s.w() if oc == 0 else []),
                    inc=[(s_ev, 1)])
                g_tsb.set(s_ev)
            ev_x1 = s_ev.count
            g_ofm.set(s_mm)
            g_af_x.set(s_ev)
            if qt == NQT - 1:
                g_wqo_busy.set(s_mm)
            tok0 = b * T + qt * 512
            d1 = x1_sp[:, :, tok0:tok0 + 512].rearrange("rc p t -> p rc t")
            ee("sync", lambda e, d1=d1: e.dma_start(
                out=d1, in_=af(4096, 4096).rearrange("p (rc t) -> p rc t", rc=RD)),
                wait=[(s_ev, ev_x1)], inc16=D_["x1s"])
            d2 = x1b_sp[:, :, tok0:tok0 + 512].rearrange("rc p t -> p rc t")
            ee("gpsimd", lambda e, d2=d2: e.dma_start(
                out=d2, in_=af(4096, 4096).rearrange("p (rc t) -> p rc t", rc=RD)),
                wait=[(s_ev, ev_x1)], inc16=D_["x1b"])
            g_af_s.set_many([D_["x1s"], D_["x1b"]])
            if b == 0 and qt == 0:
                tap("x1c0", af(4096, 4096))

    def dequant():
        dq_bar = [W(D_["x1s"]), W(D_["x1b"]), W(s_ev), W(s_mm)]
        ee("vector", None, wait=dq_bar)
        ee("sync", None, wait=dq_bar)
        RQO, SQO2, STR, STS = 1024, 2048, 3072, 6144 + 3072
        REOP = (12288, 24576)
        SEOP = (15360, 27648)
        OQP_D = (18432, 21504)
        NATP = (0, 3072)
        g_natp = [Gate(), Gate()]
        g_oqd = [Gate(), Gate()]
        g_rqs = [Gate(), Gate()]
        g_reop = [Gate(), Gate()]
        nat_i = 0
        oq_i = 0
        for wname, I_, O_, nat, wt, osp in (
                ("up", D, DF, wn_h["up"], wupT_h, wq_sp["up"]),
                ("g", D, DF, wn_h["g"], wgT_h, wq_sp["g"]),
                ("d", DF, D, wn_h["d"], wdT_h, wq_sp["d"])):
            nblk = I_ // 32
            nocn = O_ // 128
            nrc = I_ // 128
            for ocn in range(nocn):
                p = nat_i % 2
                nat_i += 1
                buf = NATP[p]
                dnat = D_["nat"] if p == 0 else D_["nat2"]
                ee("sync", lambda e, nat=nat, ocn=ocn, I_=I_, buf=buf: e.dma_start(
                    out=af(buf, I_), in_=nat[ocn * 128:(ocn + 1) * 128, :]),
                    wait=g_natp[p].w(), inc16=dnat)
                ld_n = (dnat, dnat.count)
                ee("vector", lambda e, I_=I_, nblk=nblk, buf=buf: e.reduce_max(
                    af(6144, nblk), af(buf, I_).rearrange("p (g e) -> p g e", e=32),
                    AX, apply_absolute_value=True), wait=[ld_n])
                ee("vector", lambda e, nblk=nblk: e.reciprocal(
                    af(6400, nblk), af(6144, nblk)))
                rqo = RQO + p * 256
                sqo = SQO2 + p * 256
                ee("vector", lambda e, nblk=nblk, rqo=rqo: e.tensor_single_scalar(
                    ab(rqo, 256)[:, 0:nblk], af(6400, nblk), 7.0, ALU.mult),
                    wait=g_rqs[p].w())
                ee("vector", lambda e, nblk=nblk, sqo=sqo: e.tensor_single_scalar(
                    ab(sqo, 256)[:, 0:nblk], af(6144, nblk), 1.0 / 7.0, ALU.mult),
                    inc=[(s_ev, 1)])
                evs = s_ev.count
                g_natp[p] = Gate()
                g_natp[p].set(s_ev)
                ee("tensor", None, wait=[(s_ev, evs)] + g_pzt.w())
                ee("tensor", lambda e, nblk=nblk, rqo=rqo: e.transpose(
                    pzt[0:nblk, 0:128], ab(rqo, 256)[:, 0:nblk], ident_bf[:]))
                ee("tensor", lambda e, nblk=nblk, sqo=sqo: e.transpose(
                    pzt[0:nblk, 512:640], ab(sqo, 256)[:, 0:nblk], ident_bf[:]),
                    inc=[(s_mm, 1)])
                g_rqs[p] = Gate()
                g_rqs[p].set(s_mm)
                ee("scalar", lambda e, ocn=ocn, nblk=nblk: e.copy(
                    AB[0:nblk, STR + ocn * 128:STR + (ocn + 1) * 128],
                    pzt[0:nblk, 0:128]), wait=[W(s_mm)])
                ee("scalar", lambda e, ocn=ocn, nblk=nblk: e.copy(
                    AB[0:nblk, STS + ocn * 128:STS + (ocn + 1) * 128],
                    pzt[0:nblk, 512:640]), inc=[(s_ak, 1)])
                g_pzt.set(s_ak)
            for rc in range(nrc):
                p = nat_i % 2
                nat_i += 1
                buf = NATP[p]
                dwt = D_["wt"] if p == 0 else D_["wt2"]
                ee("sync", lambda e, wt=wt, rc=rc, O_=O_, buf=buf: e.dma_start(
                    out=af(buf, O_), in_=wt[rc * 128:(rc + 1) * 128, :]),
                    wait=g_natp[p].w(), inc16=dwt)
                ld_w = (dwt, dwt.count)
                blk = (4 * rc) // 32 * 32
                kloc = 4 * rc - blk
                reo = REOP[p]
                seo = SEOP[p]
                first_sc = True
                for (srccol, dsto) in ((STR, reo), (STS, seo)):
                    for oh in range(O_ // 512):
                        slot = s4[:, (oh % 2) * 512:(oh % 2) * 512 + 512]
                        ee("tensor", None,
                           wait=g_s4[oh % 2].w() + [W(s_ak)] + g_idents.w())
                        ee("tensor", lambda e, srccol=srccol, oh=oh, blk=blk, kloc=kloc, slot=slot: e.matmul(
                            slot,
                            evar3[blk:blk + 32, (kloc // 4) * 128:(kloc // 4) * 128 + 128],
                            AB[blk:blk + 32, srccol + oh * 512:srccol + (oh + 1) * 512],
                            start=True, stop=True), inc=[(s_mm, 1)])
                        mve = s_mm.count
                        if oh % 2 == 0:
                            ee("vector", lambda e, dsto=dsto, oh=oh, slot=slot: e.tensor_copy(
                                ab(dsto, 3072)[:, oh * 512:(oh + 1) * 512], slot),
                                wait=[(s_mm, mve)], inc=[(s_ev, 1)])
                            g_s4[oh % 2].sc = [(s_ev, s_ev.count)]
                        else:
                            ee("scalar", lambda e, dsto=dsto, oh=oh, slot=slot: e.copy(
                                ab(dsto, 3072)[:, oh * 512:(oh + 1) * 512], slot),
                                wait=[(s_mm, mve)]
                                     + (g_reop[p].w() if first_sc else []),
                                inc=[(s_ak, 1)])
                            first_sc = False
                            g_s4[oh % 2].sc = [(s_ak, s_ak.count)]
                ev_exp = s_ev.count
                ak_exp = s_ak.count
                po = oq_i % 2
                oq_i += 1
                oqo = OQP_D[po]
                dq = D_["qw"] if po == 0 else D_["qw2"]
                for oh in range(O_ // 512):
                    ip = oh % 2
                    ee("vector", lambda e, oh=oh, ip=ip, buf=buf, reo=reo: e.tensor_tensor(
                        AI[:, ip * 512:(ip + 1) * 512],
                        af(buf, 3072)[:, oh * 512:(oh + 1) * 512],
                        ab(reo, 3072)[:, oh * 512:(oh + 1) * 512], ALU.mult),
                        wait=[ld_w]
                             + ([(s_ev, ev_exp), (s_ak, ak_exp)] if oh == 0 else [])
                             + (g_oqd[po].w() if oh == 0 else []))
                    ee("vector", lambda e, oh=oh, ip=ip, oqo=oqo, seo=seo: e.tensor_tensor(
                        ab(oqo, 3072)[:, oh * 512:(oh + 1) * 512],
                        AI[:, ip * 512:(ip + 1) * 512],
                        ab(seo, 3072)[:, oh * 512:(oh + 1) * 512], ALU.mult),
                        inc=[(s_ev, 1)])
                evq = s_ev.count
                g_natp[p] = Gate()
                g_natp[p].set(s_ev)
                g_reop[p] = Gate()
                g_reop[p].set(s_ev)
                ee("sync", lambda e, osp=osp, rc=rc, O_=O_, oqo=oqo: e.dma_start(
                    out=osp[rc, :, :], in_=ab(oqo, O_)),
                    wait=[(s_ev, evq)], inc16=dq)
                g_oqd[po] = Gate()
                g_oqd[po].set(dq)

    def allreduce_max(idx, srccols, dstcols, n):
        ee("gpsimd", lambda e, srccols=srccols, n=n, idx=idx: e.dma_start(
            out=cc_in[idx].ap()[:, 0:n], in_=AFr[0:1, srccols:srccols + n]),
            wait=[W(s_ev)], inc16=D_["cc"])
        ee("gpsimd", lambda e, idx=idx: e.collective_compute(
            "AllReduce", ALU.max, replica_groups=[list(range(8))],
            ins=[cc_in[idx].ap().opt()], outs=[cc_out[idx].ap().opt()]),
            wait=[W(D_["cc"])], inc=[(s_cc, 1)])
        ee("gpsimd", lambda e, dstcols=dstcols, n=n, idx=idx: e.dma_start(
            out=AFr[0:1, dstcols:dstcols + n], in_=cc_out[idx].ap()[:, 0:n]),
            wait=[W(s_cc)], inc16=D_["cc"])

    def finalize_max(mcol, dstcol):
        ee("tensor", None, wait=[W(s_gp), W(s_ev)] + g_ptf.w())
        ee("tensor", lambda e, mcol=mcol: e.transpose(
            ptf[0:1, 0:128], macc[:, mcol:mcol + 1], ident_f[:]),
            inc=[(s_mm, 1)])
        mmt = s_mm.count
        ee("scalar", lambda e: e.copy(
            AFr[0:1, 3900:4028], ptf[0:1, 0:128]),
            wait=[(s_mm, mmt)], inc=[(s_ak, 1)])
        ak_c = s_ak.count
        ee("vector", lambda e, dstcol=dstcol: e.reduce_max(
            AFr[0:1, dstcol:dstcol + 1], AFr[0:1, 3900:4028], AX,
            apply_absolute_value=True), wait=[(s_ak, ak_c)], inc=[(s_ev, 1)])
        g_ptf.set(s_ev)

    def scales_from(idx, maxcols, n, scal_base):
        for i in range(n):
            mc = maxcols + i
            ee("vector", lambda e, mc=mc, i=i: e.tensor_single_scalar(
                AFr[0:1, 3100 + 2 * i + 1:3100 + 2 * i + 2],
                AFr[0:1, mc:mc + 1], 1.0 / QMAX, ALU.mult),
                wait=[W(D_["cc"])])
            ee("vector", lambda e, i=i: e.reciprocal(
                AFr[0:1, 3100 + 2 * i:3100 + 2 * i + 1],
                AFr[0:1, 3100 + 2 * i + 1:3100 + 2 * i + 2]), inc=[(s_ev, 1)])
        evr = s_ev.count
        ee("tensor", None, wait=[(s_ev, evr)] + g_poa.w())
        ee("tensor", lambda e, n=n: e.matmul(
            poa[:, 0:2 * n], one_f[0:1, :], AFr[0:1, 3100:3100 + 2 * n],
            start=True, stop=True), inc=[(s_mm, 1)])
        mmb = s_mm.count
        ee("vector", lambda e, n=n, scal_base=scal_base: e.tensor_copy(
            sscal[:, scal_base:scal_base + 2 * n], poa[:, 0:2 * n]),
            wait=[(s_mm, mmb)], inc=[(s_ev, 1)])
        g_poa.set(s_ev)

    def ffn():
        BD = (0, 4096)
        SQF, Z2B, H2S = 8192, 12288, 16384
        g_bd = [Gate(), Gate()]
        g_z2b = Gate()
        for tc in range(NTC):
            b = tc // NTCB
            bd = BD[tc % 2]
            srcb = x1b_sp[:, :, tc * 512:(tc + 1) * 512].rearrange("rc p t -> p rc t")
            dba = D_["ba0"] if tc % 2 == 0 else D_["ba1"]
            ee("sync", lambda e, srcb=srcb, bd=bd: e.dma_start(
                out=ab(bd, 4096).rearrange("p (rc t) -> p rc t", rc=RD), in_=srcb),
                wait=g_bd[tc % 2].w(), inc16=dba)
            ld_b = (dba, dba.count)
            for rc in range(RD):
                ee("scalar", lambda e, bd=bd, rc=rc: e.activation(
                    ab(SQF, 4096)[:, rc * 512:(rc + 1) * 512],
                    ab(bd, 4096)[:, rc * 512:(rc + 1) * 512], AF.Square),
                    wait=([ld_b] if rc == 0 else []), inc=[(s_ak, 1)])
            ak_sqf = s_ak.count
            ee("tensor", None,
               wait=[ld_b, (s_ak, ak_sqf)] + g_poa.w() + g_pob.w())
            for rc in range(RD):
                ee("tensor", lambda e, bd=bd, rc=rc: e.matmul(
                    poa[0:1, :], ones_bf[:, 0:1],
                    ab(bd, 4096)[:, rc * 512:(rc + 1) * 512],
                    start=(rc == 0), stop=(rc == RD - 1)))
            for rc in range(RD):
                ee("tensor", lambda e, rc=rc: e.matmul(
                    pob[0:1, :], ones_bf[:, 0:1],
                    ab(SQF, 4096)[:, rc * 512:(rc + 1) * 512],
                    start=(rc == 0), stop=(rc == RD - 1)))
            ee("tensor", None, inc=[(s_mm, 1)])
            mm_st = s_mm.count
            ee("scalar", lambda e: e.copy(AFr[0:1, 2560:3072], poa[0:1, :]),
               wait=[(s_mm, mm_st)])
            ee("scalar", lambda e: e.copy(AFr[0:1, 2048:2560], pob[0:1, :]),
               inc=[(s_ak, 1)])
            ak_st = s_ak.count
            g_poa.set(s_ak)
            g_pob.set(s_ak)
            ee("vector", lambda e: e.tensor_single_scalar(
                AFr[0:1, 0:512], AFr[0:1, 2560:3072], 1.0 / D, ALU.mult),
                wait=[(s_ak, ak_st)])
            ee("vector", lambda e: e.tensor_tensor(
                AFr[0:1, 1024:1536], AFr[0:1, 0:512], AFr[0:1, 0:512], ALU.mult))
            ee("vector", lambda e: e.scalar_tensor_tensor(
                AFr[0:1, 1536:2048], AFr[0:1, 2048:2560], 1.0 / D,
                AFr[0:1, 1024:1536], ALU.mult, ALU.subtract), inc=[(s_ev, 1)])
            ee("scalar", lambda e: e.activation(
                AFr[0:1, 1536:2048], AFr[0:1, 1536:2048], AF.Ln,
                bias=consts[0:1, 1:2]), wait=[W(s_ev)])
            ee("scalar", lambda e: e.activation(
                AFr[0:1, 512:1024], AFr[0:1, 1536:2048], AF.Exp, scale=-0.5),
                inc=[(s_ak, 1)])
            ak_r = s_ak.count
            g_pob.set(s_ak)
            ee("tensor", None,
               wait=[(s_ak, ak_r)] + g_s4[0].w() + g_s4[1].w())
            ee("tensor", lambda e: e.matmul(
                s4[:, 0:512], one_f[0:1, :], AFr[0:1, 0:512], start=True, stop=True))
            ee("tensor", lambda e: e.matmul(
                s4[:, 512:1024], one_f[0:1, :], AFr[0:1, 512:1024],
                start=True, stop=True), inc=[(s_mm, 1)])
            mm_b = s_mm.count
            ee("vector", None, wait=[(s_mm, mm_b)] + g_z2b.w())
            for rc in range(RD):
                ee("vector", lambda e, bd=bd, rc=rc: e.tensor_tensor(
                    af(2048, 512), ab(bd, 4096)[:, rc * 512:(rc + 1) * 512],
                    s4[:, 0:512], ALU.subtract))
                ee("vector", lambda e, rc=rc: e.tensor_tensor(
                    ab(Z2B, 4096)[:, rc * 512:(rc + 1) * 512],
                    af(2048, 512), s4[:, 512:1024], ALU.mult))
                ee("vector", lambda e, rc=rc, b=b: e.tensor_scalar(
                    ab(H2S, 512), ab(Z2B, 4096)[:, rc * 512:(rc + 1) * 512],
                    smod[:, modcol(4, b, rc):modcol(4, b, rc) + 1],
                    smod[:, modcol(3, b, rc):modcol(3, b, rc) + 1],
                    ALU.mult, ALU.add), inc=[(s_ev, 1)])
                ee("vector", lambda e: e.reduce_max(
                    macc[:, 5:6], ab(H2S, 512), AX, apply_absolute_value=True))
                ee("vector", lambda e: e.tensor_tensor(
                    macc[:, 0:1], macc[:, 0:1], macc[:, 5:6], ALU.max),
                    inc=[(s_ev, 1)])
            ee("vector", None, inc=[(s_ev, 1)])
            evz2 = s_ev.count
            g_s4[0].set(s_ev)
            g_s4[1].set(s_ev)
            g_bd[tc % 2].set(s_ev)
            dz = z2_sp[:, :, tc * 512:(tc + 1) * 512].rearrange("rc p t -> p rc t")
            ee("sync", lambda e, dz=dz: e.dma_start(
                out=dz, in_=ab(Z2B, 4096).rearrange("p (rc t) -> p rc t", rc=RD)),
                wait=[(s_ev, evz2)], inc16=D_["zs"])
            g_z2b.set(D_["zs"])
            if tc == 0:
                tap("z2c0", ab(Z2B, 4096))
        finalize_max(0, 3000)
        allreduce_max(0, 3000, 3050, 1)
        scales_from(0, 3050, 1, 0)
        for b in range(B):
            for rc in range(RD):
                col = b * RD + rc
                ee("vector", lambda e, col=col, b=b, rc=rc: e.tensor_tensor(
                    qfold[:, col:col + 1],
                    smod[:, modcol(4, b, rc):modcol(4, b, rc) + 1],
                    sscal[:, 0:1], ALU.mult))
                ee("vector", lambda e, col=col, b=b, rc=rc: e.tensor_tensor(
                    qfold[:, 16 + col:16 + col + 1],
                    smod[:, modcol(3, b, rc):modcol(3, b, rc) + 1],
                    sscal[:, 0:1], ALU.mult), inc=[(s_ev, 1)])
        g_qf = Gate(); g_qf.set(s_ev)

        WUP, WG = 0, 24576
        Z2P = (49152, 53248)
        UB, GB = 57344, 63488
        ee("sync", lambda e: e.dma_start(
            out=ab(WUP, RD * DF).rearrange("p (rc o) -> p rc o", rc=RD),
            in_=wq_sp["up"].rearrange("rc p o -> p rc o")),
            wait=[W(s_ev), W(s_ak), W(s_mm), W(D_["qw"]), W(D_["qw2"]),
                  W(D_["zs"])], inc16=D_["wu"])
        ee("sync", lambda e: e.dma_start(
            out=ab(WG, RD * DF).rearrange("p (rc o) -> p rc o", rc=RD),
            in_=wq_sp["g"].rearrange("rc p o -> p rc o")), inc16=D_["wg"])
        ld_wug = [(D_["wu"], D_["wu"].count), (D_["wg"], D_["wg"].count)]
        bar_b = [W(s_mm), W(s_ev), W(s_ak), W(s_gp), W(D_["zs"])]
        ee("vector", None, wait=bar_b)
        ee("scalar", None, wait=bar_b)
        ee("gpsimd", None, wait=bar_b)
        g_z2p = [Gate(), Gate()]
        g_z2p[0].sc = list(bar_b)
        g_z2p[1].sc = list(bar_b)
        g_ub, g_gb = Gate(), Gate()
        def reload_b(tc):
            zp = Z2P[tc % 2]
            sz = z2_sp[:, :, tc * 512:(tc + 1) * 512].rearrange("rc p t -> p rc t")
            dzr = D_["zr0"] if tc % 2 == 0 else D_["zr1"]
            ee("sync", lambda e, sz=sz, zp=zp: e.dma_start(
                out=ab(zp, 4096).rearrange("p (rc t) -> p rc t", rc=RD), in_=sz),
                wait=g_z2p[tc % 2].w(), inc16=dzr)
            return (dzr, dzr.count)

        ldb = {0: reload_b(0)}
        for tc in range(NTC):
            b = tc // NTCB
            zp = Z2P[tc % 2]
            if tc + 1 < NTC:
                ldb[tc + 1] = reload_b(tc + 1)
            ld_z = ldb[tc]
            for rc in range(RD):
                ip = rc % 2
                ee("vector", lambda e, zp=zp, rc=rc, b=b, ip=ip: e.tensor_scalar(
                    AI[:, ip * 512:(ip + 1) * 512],
                    ab(zp, 4096)[:, rc * 512:(rc + 1) * 512],
                    qfold[:, b * RD + rc:b * RD + rc + 1],
                    qfold[:, 16 + b * RD + rc:16 + b * RD + rc + 1],
                    ALU.mult, ALU.add),
                    wait=[ld_z] + (g_qf.w() if tc == 0 and rc == 0 else []))
                ee("vector", lambda e, zp=zp, rc=rc, ip=ip: e.tensor_single_scalar(
                    ab(zp, 4096)[:, rc * 512:(rc + 1) * 512],
                    AI[:, ip * 512:(ip + 1) * 512], sscal[:, 1:2], ALU.mult),
                    inc=[(s_ev, 1)])
            ev_xq = s_ev.count
            if tc == 0:
                tap("xqc0", ab(Z2P[0], 4096))
            for (woff, obuf, mc, spd) in ((WUP, UB, 1, u_sp), (WG, GB, 2, g_sp)):
                for oc in range(RF):
                    pp = (0 if woff == WUP else 2) + oc % 2
                    slot = s4[:, pp * 512:(pp + 1) * 512]
                    first = (woff == WUP and oc == 0)
                    ee("tensor", None,
                       wait=(([(s_ev, ev_xq)]
                              + (ld_wug if tc == 0 else [])) if first else [])
                            + g_s4[pp].w())
                    for ic in range(RD):
                        ee("tensor", lambda e, woff=woff, ic=ic, oc=oc, zp=zp, slot=slot: e.matmul(
                            slot,
                            ab(woff, RD * DF)[:, ic * DF + oc * 128:ic * DF + (oc + 1) * 128],
                            ab(zp, 4096)[:, ic * 512:(ic + 1) * 512],
                            start=(ic == 0), stop=(ic == RD - 1)))
                    ee("tensor", None, inc=[(s_mm, 1)])
                    mmv = s_mm.count
                    oslot = (obuf, oc % 12)
                    eng = "vector" if oc % 2 == 0 else "scalar"
                    gb_ = g_ub if woff == WUP else g_gb
                    wts = [(s_mm, mmv)]
                    if oc % 12 <= 1:
                        wts += gb_.w()
                    if eng == "vector":
                        ee("vector", lambda e, obuf=obuf, oc=oc, slot=slot: e.tensor_copy(
                            ab(obuf, 6144)[:, (oc % 12) * 512:(oc % 12 + 1) * 512], slot),
                            wait=wts, inc=[(s_ev, 1)])
                        done = (s_ev, s_ev.count)
                    else:
                        ee("scalar", lambda e, obuf=obuf, oc=oc, slot=slot: e.copy(
                            ab(obuf, 6144)[:, (oc % 12) * 512:(oc % 12 + 1) * 512], slot),
                            wait=wts, inc=[(s_ak, 1)])
                        done = (s_ak, s_ak.count)
                    g_s4[pp].sc = [done]
                    if oc % 12 == 11:
                        ee("vector", lambda e, obuf=obuf: e.reduce_max(
                            macc[:, 5:6], ab(obuf, 6144), AX,
                            apply_absolute_value=True), wait=[W(s_ev), W(s_ak)])
                        ee("vector", lambda e, mc=mc: e.tensor_tensor(
                            macc[:, mc:mc + 1], macc[:, mc:mc + 1], macc[:, 5:6],
                            ALU.max), inc=[(s_ev, 1)])
                    if oc % 12 == 11:
                        half = oc // 12
                        dspd = spd[half * 12:(half + 1) * 12, :, tc * 512:(tc + 1) * 512]
                        dsp = D_["us"] if woff == WUP else D_["gs"]
                        ee("sync", lambda e, dspd=dspd, obuf=obuf: e.dma_start(
                            out=dspd.rearrange("rf p t -> p rf t"),
                            in_=ab(obuf, 6144).rearrange("p (rf t) -> p rf t", rf=12)),
                            wait=[W(s_ev), W(s_ak)], inc16=dsp)
                        gb_.set(dsp)
            g_z2p[tc % 2].set(s_mm)
        finalize_max(1, 3010)
        finalize_max(2, 3011)
        allreduce_max(1, 3010, 3052, 2)
        scales_from(1, 3052, 2, 2)

        GR = (0, 12288)
        GATEP = (24576, 36864)
        G2 = 49152
        AIG = 61440
        SIGR = 63488
        g_gr = [Gate(), Gate()]
        g_g2h = [Gate(), Gate()]
        g_gatep = [Gate(), Gate()]
        g_ai = [Gate(), Gate()]
        bar_c = [W(s_mm), W(s_ev), W(s_ak), W(s_gp), W(D_["us"]), W(D_["gs"])]
        ee("sync", None, wait=bar_c)
        ee("vector", None, wait=bar_c)
        ee("scalar", None, wait=bar_c)
        ee("gpsimd", None, wait=bar_c)
        for gg in (g_gr[0], g_gr[1], g_g2h[0], g_g2h[1], g_gatep[0],
                   g_gatep[1], g_ai[0], g_ai[1]):
            gg.sc = list(bar_c)

        def reload_c(tc):
            gr = GR[tc % 2]
            sg = g_sp[:, :, tc * 512:(tc + 1) * 512].rearrange("rf p t -> p rf t")
            dgr = D_["gr0"] if tc % 2 == 0 else D_["gr1"]
            ee("sync", lambda e, sg=sg, gr=gr: e.dma_start(
                out=ab(gr, 12288).rearrange("p (rf t) -> p rf t", rf=RF), in_=sg),
                wait=g_gr[tc % 2].w(), inc16=dgr)
            return (dgr, dgr.count)

        ldc = {0: reload_c(0)}
        for tc in range(NTC):
            gr = GR[tc % 2]
            gate = GATEP[tc % 2]
            if tc + 1 < NTC:
                ldc[tc + 1] = reload_c(tc + 1)
            ld_g = ldc[tc]
            ev_h1 = None
            tok_s = {}

            def emit_stt(rf, gate=gate):
                ip = rf % 2
                sl = slice(rf * 512, (rf + 1) * 512)
                ee("vector", lambda e, gate=gate, sl=sl, ip=ip: e.scalar_tensor_tensor(
                    ab(G2, 12288)[:, sl], AI[:, ip * 512:(ip + 1) * 512],
                    1.0 / 65536.0, ab(gate, 12288)[:, sl], ALU.mult, ALU.mult),
                    wait=[(s_ak, tok_s[rf])]
                         + (g_g2h[0].w() if rf == 0 else [])
                         + (g_g2h[1].w() if rf == 12 else []),
                    inc=[(s_ev, 1)])
                g_ai[ip] = Gate()
                g_ai[ip].set(s_ev)

            for rf in range(RF):
                ip = rf % 2
                sl = slice(rf * 512, (rf + 1) * 512)
                aig = ab(AIG + ip * 1024, 1024).bitcast(I32)
                sigr = ab(SIGR + ip * 512, 512)
                ee("vector", lambda e, gr=gr, sl=sl, aig=aig: e.tensor_single_scalar(
                    aig, ab(gr, 12288)[:, sl], sscal[:, 4:5], ALU.mult),
                    wait=([ld_g] + g_gatep[tc % 2].w() if rf == 0 else []))
                ee("vector", lambda e, gate=gate, sl=sl, aig=aig: e.tensor_single_scalar(
                    ab(gate, 12288)[:, sl], aig, sscal[:, 5:6], ALU.mult),
                    inc=[(s_ev, 1)])
                tok_g = s_ev.count
                ee("scalar", lambda e, gate=gate, sl=sl, sigr=sigr: e.activation(
                    sigr, ab(gate, 12288)[:, sl], AF.Sigmoid),
                    wait=[(s_ev, tok_g)])
                ee("scalar", lambda e, sigr=sigr, ip=ip: e.activation(
                    AI[:, ip * 512:(ip + 1) * 512], sigr, AF.Copy, scale=65536.0),
                    wait=g_ai[ip].w(), inc=[(s_ak, 1)])
                tok_s[rf] = s_ak.count
                if rf > 0:
                    emit_stt(rf - 1)
                if rf == 12:
                    ev_h1 = s_ev.count
            emit_stt(RF - 1)
            g_gr[tc % 2].set(s_ev)
            g_gatep[tc % 2] = Gate()
            g_gatep[tc % 2].set(s_ak)
            ev_done = s_ev.count
            dz1 = g2_sp[0:12, :, tc * 512:(tc + 1) * 512].rearrange("rf p t -> p rf t")
            ee("sync", lambda e, dz1=dz1: e.dma_start(
                out=dz1, in_=ab(G2, 6144).rearrange("p (rf t) -> p rf t", rf=12)),
                wait=[(s_ev, ev_h1)], inc16=D_["g2s"])
            g_g2h[0] = Gate()
            g_g2h[0].set(D_["g2s"])
            dz2 = g2_sp[12:24, :, tc * 512:(tc + 1) * 512].rearrange("rf p t -> p rf t")
            ee("sync", lambda e, dz2=dz2: e.dma_start(
                out=dz2, in_=ab(G2 + 6144, 6144).rearrange("p (rf t) -> p rf t", rf=12)),
                wait=[(s_ev, ev_done)], inc16=D_["g2s2"])
            g_g2h[1] = Gate()
            g_g2h[1].set(D_["g2s2"])
            ee("vector", lambda e: e.reduce_max(
                macc[:, 5:6], ab(G2, 12288), AX, apply_absolute_value=True))
            ee("vector", lambda e: e.tensor_tensor(
                macc[:, 3:4], macc[:, 3:4], macc[:, 5:6], ALU.max),
                inc=[(s_ev, 1)])
        finalize_max(3, 3020)
        allreduce_max(2, 3020, 3055, 1)
        scales_from(2, 3055, 1, 6)

        G2R = (0, 12288)
        UR = (24576, 36864)
        GATE2, UP = 49152, 61440
        AIGD = 73728
        g_g2r = [Gate(), Gate()]
        g_ur = [Gate(), Gate()]
        g_oph = [Gate(), Gate()]
        g_upb = Gate()
        bar_d = [W(s_mm), W(s_ev), W(s_ak), W(s_gp), W(D_["g2s"]), W(D_["g2s2"])]
        ee("sync", None, wait=bar_d)
        ee("vector", None, wait=bar_d)
        ee("scalar", None, wait=bar_d)
        ee("gpsimd", None, wait=bar_d)
        for gg in (g_g2r[0], g_g2r[1], g_ur[0], g_ur[1], g_oph[0],
                   g_oph[1], g_upb):
            gg.sc = list(bar_d)
        def reload_d(tc):
            g2r, ur = G2R[tc % 2], UR[tc % 2]
            sg = g2_sp[:, :, tc * 512:(tc + 1) * 512].rearrange("rf p t -> p rf t")
            dd = D_["dd0"] if tc % 2 == 0 else D_["dd1"]
            du = D_["du0"] if tc % 2 == 0 else D_["du1"]
            ee("sync", lambda e, sg=sg, g2r=g2r: e.dma_start(
                out=ab(g2r, 12288).rearrange("p (rf t) -> p rf t", rf=RF), in_=sg),
                wait=g_g2r[tc % 2].w(), inc16=dd)
            su = u_sp[:, :, tc * 512:(tc + 1) * 512].rearrange("rf p t -> p rf t")
            ee("sync", lambda e, su=su, ur=ur: e.dma_start(
                out=ab(ur, 12288).rearrange("p (rf t) -> p rf t", rf=RF), in_=su),
                wait=g_ur[tc % 2].w(), inc16=du)
            return [(dd, dd.count), (du, du.count)]

        ldd = {0: reload_d(0)}
        for tc in range(NTC):
            g2r, ur = G2R[tc % 2], UR[tc % 2]
            if tc + 1 < NTC:
                ldd[tc + 1] = reload_d(tc + 1)
            ld_d = ldd[tc]
            tok_up = {}

            def emit_tt(rf):
                sl = slice(rf * 512, (rf + 1) * 512)
                ee("vector", lambda e, sl=sl: e.tensor_tensor(
                    ab(GATE2, 12288)[:, sl], ab(GATE2, 12288)[:, sl],
                    ab(UP, 12288)[:, sl], ALU.mult),
                    wait=[(s_ak, tok_up[rf])], inc=[(s_ev, 1)])

            for rf in range(RF):
                ip = rf % 2
                sl = slice(rf * 512, (rf + 1) * 512)
                aig = ab(AIGD + ip * 1024, 1024).bitcast(I32)
                ee("vector", lambda e, g2r=g2r, sl=sl, aig=aig: e.tensor_single_scalar(
                    aig, ab(g2r, 12288)[:, sl], sscal[:, 6:7], ALU.mult),
                    wait=((ld_d + g_oph[0].w()) if rf == 0 else [])
                         + (g_oph[1].w() if rf == 12 else []))
                ee("vector", lambda e, sl=sl, aig=aig: e.tensor_single_scalar(
                    ab(GATE2, 12288)[:, sl], aig, sscal[:, 7:8], ALU.mult))
                ee("scalar", lambda e, ur=ur, sl=sl, ip=ip: e.activation(
                    AI[:, ip * 512:(ip + 1) * 512], ab(ur, 12288)[:, sl],
                    AF.Copy, scale=sscal[:, 2:3]),
                    wait=((ld_d + g_upb.w()) if rf == 0 else []))
                ee("scalar", lambda e, sl=sl, ip=ip: e.activation(
                    ab(UP, 12288)[:, sl], AI[:, ip * 512:(ip + 1) * 512],
                    AF.Copy, scale=sscal[:, 3:4]), inc=[(s_ak, 1)])
                tok_up[rf] = s_ak.count
                if rf > 0:
                    emit_tt(rf - 1)
                if rf == 12:
                    ev_h1 = s_ev.count
            emit_tt(RF - 1)
            g_g2r[tc % 2].set(s_ev)
            g_ur[tc % 2].set(s_ak)
            g_upb = Gate()
            g_upb.set(s_ev)
            ev_done = s_ev.count
            ee("vector", lambda e: e.reduce_max(
                macc[:, 5:6], ab(GATE2, 12288), AX, apply_absolute_value=True))
            ee("vector", lambda e: e.tensor_tensor(
                macc[:, 4:5], macc[:, 4:5], macc[:, 5:6], ALU.max),
                inc=[(s_ev, 1)])
            ev_red = s_ev.count
            do1 = op_sp[0:12, :, tc * 512:(tc + 1) * 512].rearrange("rf p t -> p rf t")
            ee("sync", lambda e, do1=do1: e.dma_start(
                out=do1, in_=ab(GATE2, 6144).rearrange("p (rf t) -> p rf t", rf=12)),
                wait=[(s_ev, ev_h1)], inc16=D_["ops"])
            g_oph[0] = Gate()
            g_oph[0].sc = [(D_["ops"], D_["ops"].count), (s_ev, ev_red)]
            do2 = op_sp[12:24, :, tc * 512:(tc + 1) * 512].rearrange("rf p t -> p rf t")
            ee("sync", lambda e, do2=do2: e.dma_start(
                out=do2, in_=ab(GATE2 + 6144, 6144).rearrange("p (rf t) -> p rf t", rf=12)),
                wait=[(s_ev, ev_done)], inc16=D_["ops2"])
            g_oph[1] = Gate()
            g_oph[1].sc = [(D_["ops2"], D_["ops2"].count), (s_ev, ev_red)]
        finalize_max(4, 3030)
        allreduce_max(3, 3030, 3057, 1)
        scales_from(3, 3057, 1, 8)

        WDo = 0
        OPR = (24576, 36864)
        OQP = (49152, 61440)
        bar_e = [W(s_mm), W(s_ev), W(s_ak), W(s_gp), W(D_["ops"]), W(D_["ops2"])]
        ee("sync", None, wait=bar_e)
        ee("vector", None, wait=bar_e)
        ee("scalar", None, wait=bar_e)
        ee("sync", lambda e: e.dma_start(
            out=ab(WDo, RF * D).rearrange("p (rf o) -> p rf o", rf=RF),
            in_=wq_sp["d"].rearrange("rf p o -> p rf o")),
            wait=[W(s_ev), W(s_mm), W(s_ak), W(s_gp), W(D_["qw"]), W(D_["qw2"]),
                  W(D_["ops"]), W(D_["ops2"])], inc16=D_["wd"])
        ld_wd = (D_["wd"], D_["wd"].count)
        g_opr = [Gate(), Gate()]
        g_oqp = [Gate(), Gate()]
        g_x1r = Gate()
        g_outstage = Gate()
        def reload_e(tc):
            opr = OPR[tc % 2]
            so = op_sp[:, :, tc * 512:(tc + 1) * 512].rearrange("rf p t -> p rf t")
            dor = D_["or0"] if tc % 2 == 0 else D_["or1"]
            ee("sync", lambda e, so=so, opr=opr: e.dma_start(
                out=ab(opr, 12288).rearrange("p (rf t) -> p rf t", rf=RF), in_=so),
                wait=g_opr[tc % 2].w(), inc16=dor)
            return (dor, dor.count)

        lde = {0: reload_e(0)}
        for tc in range(NTC):
            b = tc // NTCB
            opr = OPR[tc % 2]
            if tc + 1 < NTC:
                lde[tc + 1] = reload_e(tc + 1)
            ld_o = lde[tc]
            sx1 = x1_sp[:, :, tc * 512:(tc + 1) * 512].rearrange("rc p t -> p rc t")
            ee("sync", lambda e, sx1=sx1: e.dma_start(
                out=af(0, 4096).rearrange("p (rc t) -> p rc t", rc=RD), in_=sx1),
                wait=g_x1r.w(), inc16=D_["x1r"])
            ld_x1 = (D_["x1r"], D_["x1r"].count)
            oq = OQP[tc % 2]
            for rf in range(RF):
                ip = rf % 2
                sl = slice(rf * 512, (rf + 1) * 512)
                ee("vector", lambda e, opr=opr, sl=sl, ip=ip: e.tensor_single_scalar(
                    AI[:, ip * 512:(ip + 1) * 512], ab(opr, 12288)[:, sl],
                    sscal[:, 8:9], ALU.mult),
                    wait=([ld_o] if rf == 0 else []) + (g_oqp[tc % 2].w() if rf == 0 else []))
                ee("vector", lambda e, oq=oq, sl=sl, ip=ip: e.tensor_single_scalar(
                    ab(oq, 12288)[:, sl], AI[:, ip * 512:(ip + 1) * 512],
                    sscal[:, 9:10], ALU.mult), inc=[(s_ev, 1)])
            ev_oq = s_ev.count
            g_opr[tc % 2].set(s_ev)
            for oc in range(RD):
                pp = oc % 2
                slot = s4[:, pp * 512:(pp + 1) * 512]
                ee("tensor", None,
                   wait=(([(s_ev, ev_oq)]
                          + ([ld_wd] if tc == 0 else [])) if oc == 0 else [])
                        + g_s4[pp].w())
                for ic in range(RF):
                    ee("tensor", lambda e, oq=oq, ic=ic, oc=oc, slot=slot: e.matmul(
                        slot,
                        ab(WDo, RF * D)[:, ic * D + oc * 128:ic * D + (oc + 1) * 128],
                        ab(oq, 12288)[:, ic * 512:(ic + 1) * 512],
                        start=(ic == 0), stop=(ic == RF - 1)))
                ee("tensor", None, inc=[(s_mm, 1)])
                mmd = s_mm.count
                ee("vector", lambda e, oc=oc, b=b, slot=slot: e.scalar_tensor_tensor(
                    af(0, 4096)[:, oc * 512:(oc + 1) * 512], slot,
                    smod[:, modcol(5, b, oc):modcol(5, b, oc) + 1],
                    af(0, 4096)[:, oc * 512:(oc + 1) * 512],
                    ALU.mult, ALU.add),
                    wait=[(s_mm, mmd), ld_x1], inc=[(s_ev, 1)])
                g_s4[pp].set(s_ev)
            ev_x2 = s_ev.count
            g_oqp[tc % 2] = Gate()
            g_oqp[tc % 2].set(s_mm)
            for tt in range(4):
                for half in range(2):
                    ee("tensor", None,
                       wait=[(s_ev, ev_x2)] + g_ptf.w())
                    for r4 in range(4):
                        rc = half * 4 + r4
                        ee("tensor", lambda e, tt=tt, rc=rc, r4=r4: e.transpose(
                            ptf[:, r4 * 128:(r4 + 1) * 128],
                            af(0, 4096)[:, rc * 512 + tt * 128:rc * 512 + (tt + 1) * 128],
                            ident_f[:]))
                    ee("tensor", None, inc=[(s_mm, 1)])
                    mmt = s_mm.count
                    ee("scalar", lambda e, tt=tt, half=half: e.copy(
                        af(4096, 4096)[:, tt * D + half * 512:tt * D + (half + 1) * 512],
                        ptf[:, :]),
                        wait=[(s_mm, mmt)] + (g_outstage.w() if tt == 0 and half == 0 else []),
                        inc=[(s_ak, 1)])
                    g_ptf.set(s_ak)
            ev_out = s_ak.count
            g_x1r.set(s_mm)
            dst = out_hbm[b, (tc % NTCB) * 512:(tc % NTCB + 1) * 512, :].rearrange(
                "(tt p) f -> p tt f", p=128)
            ee("sync", lambda e, dst=dst: e.dma_start(
                out=dst, in_=af(4096, 4096).rearrange("p (tt f) -> p tt f", tt=4)),
                wait=[(s_ak, ev_out)], inc16=D_["out"])
            g_outstage.set(D_["out"])

    seq = [lambda: stage1(0), lambda: stage2(0), lambda: stage1(1),
           lambda: stage2(1), dequant, ffn]
    for i, fn in enumerate(seq):
        if i < PHASES:
            fn()
    ee("sync", None, wait=[W(D_["out"]), W(D_["tap"])])

    return _finalize(nc, kb)


def _finalize(nc, kb):
    stack = ExitStack()
    for s in kb.sems:
        s.handle = stack.enter_context(nc.semaphore(s.name))
    for e in SELF_CHAIN:
        ss = Sem(f"self_{e}")
        ss.handle = stack.enter_context(nc.semaphore(ss.name))
        kb.selfsems[e] = ss
    stack.enter_context(nc.allow_low_precision(
        reason="bf16 matmul operands / bf16 softmax reciprocal are deliberate"))
    with nc.Block() as block:
        @block.sync
        def _(e):
            kb.replay("sync", e)

        @block.scalar
        def _(e):
            kb.replay("scalar", e)

        @block.vector
        def _(e):
            kb.replay("vector", e)

        @block.tensor
        def _(e):
            kb.replay("tensor", e)

        @block.gpsimd
        def _(e):
            kb.replay("gpsimd", e)
    stack.close()
    return nc



def kernel(**inputs):
    f32 = np.float32
    x = np.ascontiguousarray(inputs["L0_attn_x"], dtype=f32)
    biases = np.ascontiguousarray(np.stack([
        inputs["L0_attn_bq"], inputs["L0_attn_bk"],
        inputs["L0_attn_bv"], inputs["L0_attn_bo"]]).astype(f32))
    base = {
        "wqT": np.ascontiguousarray(inputs["L0_attn_Wq"].astype(f32).T),
        "wkT": np.ascontiguousarray(inputs["L0_attn_Wk"].astype(f32).T),
        "wvT": np.ascontiguousarray(inputs["L0_attn_Wv"].astype(f32).T),
        "woT": np.ascontiguousarray(inputs["L0_attn_Wo"].astype(f32).T),
        "biases": biases,
        "wupT": np.ascontiguousarray(inputs["L0_ffn_w_up"].astype(f32).T),
        "wgT": np.ascontiguousarray(inputs["L0_ffn_w_gate"].astype(f32).T),
        "wdT": np.ascontiguousarray(inputs["L0_ffn_w_down"].astype(f32).T),
        "wup_n": np.ascontiguousarray(inputs["L0_ffn_w_up"].astype(f32)),
        "wg_n": np.ascontiguousarray(inputs["L0_ffn_w_gate"].astype(f32)),
        "wd_n": np.ascontiguousarray(inputs["L0_ffn_w_down"].astype(f32)),
        "wmod": np.ascontiguousarray(inputs["L0_wmod"].astype(f32)),
    }
    cc = np.ascontiguousarray(inputs["L0_c"], dtype=f32)
    nc_obj = build_nc(n_dbg=max(list(DBG.values()) + [-1]) + 1)
    in_maps = []
    for core in range(8):
        m = dict(base)
        m["x"] = np.ascontiguousarray(x[2 * core:2 * core + 2])
        m["c"] = np.ascontiguousarray(cc[2 * core:2 * core + 2])
        in_maps.append(m)
    res = run_bass_kernel_spmd(nc_obj, in_maps, core_ids=list(range(8)))
    out = np.concatenate([res.results[c]["out"] for c in range(8)], axis=0)
    if DBG:
        kernel.dbg = [res.results[0][f"dbg{i}"]
                      for i in range(max(DBG.values()) + 1)]
    return out
